# version 04
# baseline (speedup 1.0000x reference)
import numpy as np
import ml_dtypes
from contextlib import ExitStack

import concourse.bass as bass
import concourse.mybir as mybir
from concourse.bass_utils import run_bass_kernel_spmd

F32 = mybir.dt.float32
BF16 = mybir.dt.bfloat16
AF = mybir.ActivationFunctionType
ALU = mybir.AluOpType
AX = mybir.AxisListType
bfdt = ml_dtypes.bfloat16

P = 128
D = 2048
T = 2048
HID = 8192
ND = D // P
NE = D // P
NT = T // P
NH = HID // P
SCALE = float(np.float32(128 ** -0.5))
MAGIC = float(np.float32(1.5 * 2 ** 23))
EPS_LN = 1e-5
EPS_RMS = float(np.float32(np.finfo(np.float32).eps))
INV_D = 1.0 / 2048.0
INV_H = 1.0 / 8192.0
INV_G = 1.0 / 64.0

ENGINES = ("sync", "scalar", "vector", "tensor", "gpsimd")
QSEM = {"sync": "ds", "scalar": "dc", "gpsimd": "dg"}
ESEM = {"tensor": "pe", "vector": "dv", "scalar": "ac", "gpsimd": "po"}


class Buf:
    __slots__ = ("w", "readers")

    def __init__(self):
        self.w = None
        self.readers = []


class Plan:
    def __init__(self):
        self.q = {e: [] for e in ENGINES}
        self.cnt = {}
        self.sat = {e: {} for e in ENGINES}

    def _wait(self, eng, sem, val):
        if val <= 0 or self.sat[eng].get(sem, 0) >= val:
            return
        self.sat[eng][sem] = val
        self.q[eng].append(("w", sem, val))

    def emit(self, eng, builders, reads=(), writes=(), sem=None, inc=1):
        need = {}
        for b in reads:
            if b.w:
                s, v = b.w
                need[s] = max(need.get(s, 0), v)
        for b in writes:
            if b.w:
                s, v = b.w
                need[s] = max(need.get(s, 0), v)
            for s, v in b.readers:
                need[s] = max(need.get(s, 0), v)
        for s, v in need.items():
            self._wait(eng, s, v)
        if sem is None:
            sem = QSEM.get(eng) if False else ESEM.get(eng)
        self.cnt[sem] = self.cnt.get(sem, 0) + inc
        tok = (sem, self.cnt[sem])
        if not isinstance(builders, (list, tuple)):
            builders = [builders]
        for i, b in enumerate(builders):
            if i == len(builders) - 1:
                self.q[eng].append(("i", b, sem, inc))
            else:
                self.q[eng].append(("p", b))
        for b in writes:
            b.w = tok
            b.readers = []
        for b in reads:
            b.readers.append(tok)
        return tok

    def dma(self, queue, out_ap_f, in_ap_f, reads=(), writes=(), **kw):
        sem = QSEM[queue]
        return self.emit(
            queue,
            [lambda E, o=out_ap_f, i=in_ap_f, kw=kw: E.dma_start(out=o(), in_=i(), **kw)],
            reads=reads, writes=writes, sem=sem, inc=16,
        )

    def barrier(self):
        snap = dict(self.cnt)
        for eng in ENGINES:
            for s, v in snap.items():
                self._wait(eng, s, v)


def build_nc():
    nc = bass.Bass("TRN2", target_bir_lowering=False, detect_race_conditions=False)
    pl = Plan()

    x_in = nc.dram_tensor("x", [T, D], F32, kind="ExternalInput").ap()
    w_in = {n: nc.dram_tensor("w" + n, [D, D], F32, kind="ExternalInput").ap()
            for n in ("q", "k", "v", "o")}
    b_in = {n: nc.dram_tensor("b" + n, [1, D], F32, kind="ExternalInput").ap()
            for n in ("q", "k", "v", "o")}
    wfc_in = nc.dram_tensor("wfc", [HID, D], F32, kind="ExternalInput").ap()
    wpr_in = nc.dram_tensor("wpr", [D, HID], F32, kind="ExternalInput").ap()
    mask_in = nc.dram_tensor("mask", [P, 8], F32, kind="ExternalInput").ap()
    out_hbm = nc.dram_tensor("out", [T, D], F32, kind="ExternalOutput").ap()

    xb = nc.dram_tensor("xb", [T, D], BF16).ap()
    wb = {n: nc.dram_tensor("wb" + n, [D, D], BF16).ap() for n in ("q", "k", "v", "o")}
    wT = {n: nc.dram_tensor("wT" + n, [D, D], BF16).ap() for n in ("q", "k", "v", "o")}
    w1n = nc.dram_tensor("w1n", [HID, D], BF16).ap()
    w2n = nc.dram_tensor("w2n", [D, HID], BF16).ap()
    wfcT = nc.dram_tensor("wfcT", [D, HID], BF16).ap()
    wprT = nc.dram_tensor("wprT", [HID, D], BF16).ap()
    cloc = nc.dram_tensor("cloc", [D, T], BF16).ap()
    phq = nc.dram_tensor("phq", [D, T], BF16).ap()
    y1f = nc.dram_tensor("y1f", [T, D], F32).ap()
    y1b = nc.dram_tensor("y1b", [T, D], BF16).ap()
    y1T = nc.dram_tensor("y1T", [D, T], BF16).ap()
    agi = nc.dram_tensor("agi", [P, 16], F32).ap()
    ago = nc.dram_tensor("ago", [8 * P, 16], F32).ap()

    st = ExitStack()
    e = st.enter_context

    sems = {}
    for s in ("pe", "dv", "ac", "po", "cc", "ds", "dc", "dg"):
        sems[s] = e(nc.semaphore(s))

    bounce = [e(nc.sbuf_tensor(f"bounce{i}", [P, D], BF16)) for i in range(2)]
    qw_wb = [e(nc.sbuf_tensor(f"qwb{i}", [P, D], BF16)) for i in range(2)]
    qw_t = [e(nc.sbuf_tensor(f"qwt{i}", [P, D], BF16)) for i in range(2)]
    qgs = [e(nc.sbuf_tensor(f"qgs{i}", [P, 32], F32)) for i in range(2)]
    qsc = [e(nc.sbuf_tensor(f"qsc{i}", [P, 32], BF16)) for i in range(2)]
    qrc = [e(nc.sbuf_tensor(f"qrc{i}", [P, 32], F32)) for i in range(2)]
    totals = e(nc.sbuf_tensor("totals", [P, 16], F32))
    agt = e(nc.sbuf_tensor("agt", [P, 8 * 16], F32))
    maskb = e(nc.sbuf_tensor("maskb", [P, 8], F32))
    maskd = e(nc.sbuf_tensor("maskd", [P, 16 * 8], F32))
    pcol = e(nc.sbuf_tensor("pcol", [P, 16], F32))
    ones_row = e(nc.sbuf_tensor("ones_row", [1, 512], BF16))
    ones128 = e(nc.sbuf_tensor("ones128", [P, P], BF16))
    ones1f = e(nc.sbuf_tensor("ones1f", [1, 1], F32))
    c_eln = e(nc.sbuf_tensor("c_eln", [P, 1], F32))
    c_erm = e(nc.sbuf_tensor("c_erm", [P, 1], F32))

    pA = e(nc.psum_tensor("pA", [P, D], F32))
    pB = e(nc.psum_tensor("pB", [P, D], F32))
    pbank = [Buf() for _ in range(8)]

    def psum_view(which, lo, hi):
        t_ = pA if which == "A" else pB
        return t_[:, lo * 512:hi * 512]

    def banks(which, lo, hi):
        off = 0 if which == "A" else 4
        return [pbank[off + i] for i in range(lo, hi)]

    B_ = {}

    def buf(name):
        if name not in B_:
            B_[name] = Buf()
        return B_[name]

    pl.emit("vector", [
        lambda E: E.memset(ones_row[:], 1.0),
        lambda E: E.memset(ones128[:], 1.0),
        lambda E: E.memset(ones1f[:], 1.0),
        lambda E: E.memset(c_eln[:], EPS_LN),
        lambda E: E.memset(c_erm[:], EPS_RMS),
    ], writes=[buf("consts")], sem="dv")
    pl.dma("scalar", lambda: maskb[:], lambda: mask_in, writes=[buf("maskb")])

    brow = {n: e(nc.sbuf_tensor(f"brow{n}", [1, D], BF16)) for n in ("q", "k", "v", "o")}
    for n in ("q", "k", "v", "o"):
        pl.dma("gpsimd", lambda n=n: brow[n][:], lambda n=n: b_in[n], writes=[buf("brow" + n)])

    CAST_CH = 4
    for n in ("k", "v", "q", "o"):
        for ci in range(CAST_CH):
            r0, r1 = ci * (D // CAST_CH), (ci + 1) * (D // CAST_CH)
            pl.dma("gpsimd", lambda n=n, r0=r0, r1=r1: wb[n][r0:r1, :],
                   lambda n=n, r0=r0, r1=r1: w_in[n][r0:r1, :],
                   writes=[buf(f"wb{n}")])
        if n == "v":
            for ci in range(CAST_CH):
                r0, r1 = ci * (T // CAST_CH), (ci + 1) * (T // CAST_CH)
                pl.dma("gpsimd", lambda r0=r0, r1=r1: xb[r0:r1, :],
                       lambda r0=r0, r1=r1: x_in[r0:r1, :],
                       writes=[buf("xb")])

    scopeA = ExitStack()
    eA = scopeA.enter_context
    AR1 = eA(nc.sbuf_tensor("AR1", [P, ND * T], BF16))

    def pretranspose_weight(n):
        for dc in range(ND):
            bb = bounce[dc % 2]
            bbuf = buf(f"bounce{dc % 2}")
            pl.dma("sync",
                   lambda bb=bb: bb[:],
                   lambda n=n, dc=dc: wb[n][:, dc * P:(dc + 1) * P],
                   reads=[buf(f"wb{n}")], writes=[bbuf], transpose=True)
            pl.dma("scalar",
                   lambda n=n, dc=dc: wT[n][dc * P:(dc + 1) * P, :],
                   lambda bb=bb: bb[:],
                   reads=[bbuf], writes=[buf(f"wT{n}")])

    pretranspose_weight("k")
    pretranspose_weight("v")
    for dc in range(ND):
        pl.dma("sync",
               lambda dc=dc: AR1[:, dc * T:(dc + 1) * T],
               lambda dc=dc: xb[:, dc * P:(dc + 1) * P],
               reads=[buf("xb")], writes=[buf(f"ar1_{dc}")], transpose=True)
    pretranspose_weight("q")
    pretranspose_weight("o")

    qcount = [0]

    def quant_stage1(src_ap_f, i):
        wbb, tb = qw_wb[i], qw_t[i]
        gs, sc, rc = qgs[i], qsc[i], qrc[i]
        wbu, tbu = buf(f"qwb{i}"), buf(f"qwt{i}")
        gsu, scu, rcu = buf(f"qgs{i}"), buf(f"qsc{i}"), buf(f"qrc{i}")
        pl.dma("gpsimd", lambda wbb=wbb: wbb[:], src_ap_f, writes=[wbu])
        pl.emit("vector", [lambda E, wbb=wbb, gs=gs: E.tensor_reduce(
            out=gs.ap().unsqueeze(2),
            in_=wbb.ap().rearrange("p (g j) -> p g j", j=64),
            axis=AX.X, op=ALU.add, apply_absolute_value=True)],
            reads=[wbu], writes=[gsu])
        pl.emit("vector", [lambda E, gs=gs, sc=sc: E.tensor_scalar(
            out=sc[:], in0=gs[:], scalar1=INV_G, scalar2=1e-8,
            op0=ALU.mult, op1=ALU.max)], reads=[gsu], writes=[scu])
        pl.emit("vector", [lambda E, sc=sc, rc=rc: E.reciprocal(out=rc[:], in_=sc[:])],
                reads=[scu], writes=[rcu])
        pl.emit("gpsimd", [lambda E, wbb=wbb, rc=rc, tb=tb: E.tensor_tensor(
            out=tb.ap().rearrange("p (g j) -> p g j", j=64),
            in0=wbb.ap().rearrange("p (g j) -> p g j", j=64),
            in1=rc.ap().unsqueeze(2).broadcast_to([P, 32, 64]),
            op=ALU.mult)], reads=[wbu, rcu], writes=[tbu], sem="po")

    def quant_stage2(dst_ap_f, tag, i):
        wbb, tb = qw_wb[i], qw_t[i]
        sc = qsc[i]
        wbu, tbu, scu = buf(f"qwb{i}"), buf(f"qwt{i}"), buf(f"qsc{i}")
        pl.emit("vector", [lambda E, tb=tb: E.tensor_scalar(
            out=tb[:], in0=tb[:], scalar1=MAGIC, scalar2=MAGIC,
            op0=ALU.add, op1=ALU.subtract)], reads=[tbu], writes=[tbu])
        pl.emit("vector", [lambda E, tb=tb: E.tensor_scalar(
            out=tb[:], in0=tb[:], scalar1=1.0, scalar2=-1.0,
            op0=ALU.min, op1=ALU.max)], reads=[tbu], writes=[tbu])
        pl.emit("vector", [lambda E, tb=tb, sc=sc, wbb=wbb: E.tensor_tensor(
            out=wbb.ap().rearrange("p (g j) -> p g j", j=64),
            in0=tb.ap().rearrange("p (g j) -> p g j", j=64),
            in1=sc.ap().unsqueeze(2).broadcast_to([P, 32, 64]),
            op=ALU.mult)], reads=[tbu, scu], writes=[wbu])
        pl.dma("scalar", dst_ap_f, lambda wbb=wbb: wbb[:],
               reads=[wbu], writes=[buf(tag)])

    fc_blocks = [(lambda r=r: wfc_in[r * P:(r + 1) * P, :],
                  lambda r=r: w1n[r * P:(r + 1) * P, :]) for r in range(NH)]
    pr_blocks = []
    for hq in range(4):
        for r in range(ND):
            pr_blocks.append(
                (lambda r=r, hq=hq: wpr_in[r * P:(r + 1) * P, hq * D:(hq + 1) * D],
                 lambda r=r, hq=hq: w2n[r * P:(r + 1) * P, hq * D:(hq + 1) * D]))
    quant_todo = [(s, d, "w1n_h%d" % (ix // 16)) for ix, (s, d) in enumerate(fc_blocks)] + \
                 [(s, d, "w2n_h%d" % (ix // 16)) for ix, (s, d) in enumerate(pr_blocks)]
    quant_s1 = [0]
    quant_s2 = [0]

    def quant_pump(k):
        for _ in range(k):
            if quant_s1[0] < len(quant_todo):
                i = quant_s1[0]
                quant_stage1(quant_todo[i][0], i % 2)
                quant_s1[0] += 1
            if quant_s2[0] < quant_s1[0] - 1 or (
                    quant_s1[0] == len(quant_todo) and quant_s2[0] < quant_s1[0]):
                i = quant_s2[0]
                quant_stage2(quant_todo[i][1], quant_todo[i][2], i % 2)
                quant_s2[0] += 1
            pretr_pump()

    def quant_flush():
        while quant_s1[0] < len(quant_todo) or quant_s2[0] < len(quant_todo):
            if quant_s1[0] < len(quant_todo):
                i = quant_s1[0]
                quant_stage1(quant_todo[i][0], i % 2)
                quant_s1[0] += 1
            if quant_s2[0] < quant_s1[0]:
                i = quant_s2[0]
                quant_stage2(quant_todo[i][1], quant_todo[i][2], i % 2)
                quant_s2[0] += 1
            pretr_pump()

    pretr_todo = []
    for hq in range(4):
        for dc in range(ND):
            pretr_todo.append((16 * (hq + 1),
                lambda dc=dc, hq=hq: w1n[hq * D:(hq + 1) * D, dc * P:(dc + 1) * P],
                lambda dc=dc, hq=hq: wfcT[dc * P:(dc + 1) * P, hq * D:(hq + 1) * D],
                "w1n_h%d" % hq, "wfcT"))
    for hq in range(4):
        for hcr in range(16):
            hc = hq * 16 + hcr
            pretr_todo.append((64 + 16 * (hq + 1),
                lambda hc=hc: w2n[:, hc * P:(hc + 1) * P],
                lambda hc=hc: wprT[hc * P:(hc + 1) * P, :],
                "w2n_h%d" % hq, "wprT"))
    pretr_ix = [0]
    pretr_n = [0]

    def pretr_pump():
        while pretr_ix[0] < len(pretr_todo) and \
                pretr_todo[pretr_ix[0]][0] <= quant_s2[0]:
            _, inf, outf, stag, dtag = pretr_todo[pretr_ix[0]]
            pretr_ix[0] += 1
            bb = bounce[pretr_n[0] % 2]
            bbuf = buf(f"bounce{pretr_n[0] % 2}")
            pretr_n[0] += 1
            pl.dma("sync", lambda bb=bb: bb[:], inf,
                   reads=[buf(stag)], writes=[bbuf], transpose=True)
            pl.dma("sync", outf, lambda bb=bb: bb[:],
                   reads=[bbuf], writes=[buf(dtag)])

    scopeA1 = ExitStack()
    e1 = scopeA1.enter_context
    wstream = {n: e1(nc.sbuf_tensor(f"wst{n}", [P, 2 * ND * 256], BF16))
               for n in ("k", "v", "q")}
    r_t = e1(nc.sbuf_tensor("r_t", [P, T], F32))
    m_t = e1(nc.sbuf_tensor("m_t", [P, T], F32))
    S_t = e1(nc.sbuf_tensor("S_t", [P, T], BF16))
    C_t = [e1(nc.sbuf_tensor(f"C_t{i}", [P, T], BF16)) for i in range(2)]
    ph_t = [e1(nc.sbuf_tensor(f"ph_t{i}", [P, T], BF16)) for i in range(2)]

    def wst_view(n, slot, dc, e0, width):
        base = slot * ND * 256 + dc * 256
        return wstream[n][:, base + e0: base + e0 + width]

    def load_eighth(n, slot, et0):
        pl.dma("scalar",
               lambda n=n, slot=slot: wstream[n][:, slot * ND * 256:(slot + 1) * ND * 256]
               .rearrange("p (dc e) -> p dc e", e=256),
               lambda n=n, et0=et0: wT[n].rearrange("(dc p) e -> p dc e", p=P)
               [:, :, et0 * P:(et0 + 2) * P],
               reads=[buf(f"wT{n}")], writes=[buf(f"wst{n}{slot}")])

    for n in ("k", "v", "q"):
        load_eighth(n, 0, 0)

    def proj_mms(et, n, pwhich, phi_scale, extra_reads):
        slot = (et // 2) % 2
        erel = (et % 2) * P
        wsbuf = buf(f"wst{n}{slot}")
        mms = []
        for dc in range(ND):
            for tn in range(4):
                mms.append(lambda E, n=n, slot=slot, dc=dc, erel=erel, tn=tn,
                           pw=pwhich: E.matmul(
                               psum_view(pw, tn, tn + 1),
                               wst_view(n, slot, dc, erel, P),
                               AR1[:, dc * T + tn * 512: dc * T + (tn + 1) * 512],
                               start=(dc == 0), stop=False))
        for tn in range(4):
            mms.append(lambda E, n=n, et=et, tn=tn, pw=pwhich: E.matmul(
                psum_view(pw, tn, tn + 1),
                brow[n][:, et * P:(et + 1) * P],
                ones_row[:],
                start=False, stop=True))
        reads = [wsbuf, buf("brow" + n), buf("consts")] + \
                [buf(f"ar1_{dc}") for dc in range(ND)] + extra_reads
        pl.emit("tensor", mms, reads=reads, writes=banks(pwhich, 0, 4), sem="pe")

    def phi_evac(pwhich, out_tile, out_buf, s):
        pb = banks(pwhich, 0, 4)
        pl.emit("scalar", [lambda E, pw=pwhich, s=s: E.activation(
            out=r_t[:], in_=psum_view(pw, 0, 4), func=AF.Relu, scale=s)],
            reads=pb, writes=[buf("r_t")])
        pl.emit("vector", [lambda E, pw=pwhich: E.tensor_scalar(
            out=m_t[:], in0=psum_view(pw, 0, 4), scalar1=0.0, scalar2=None,
            op0=ALU.min)], reads=pb + [buf("r_t")], writes=[buf("m_t")])
        pl.emit("scalar", [lambda E: E.activation(
            out=m_t[:], in_=m_t[:], func=AF.Exp)],
            reads=[buf("m_t")], writes=[buf("m_t")])
        pl.emit("vector", [lambda E, out_tile=out_tile, s=s: E.scalar_tensor_tensor(
            out=out_tile[:], in0=m_t[:], scalar=s, in1=r_t[:],
            op0=ALU.mult, op1=ALU.add)],
            reads=[buf("m_t"), buf("r_t")], writes=[out_buf])

    for et in range(NE):
        pk = "A" if et % 2 == 0 else "B"
        pv = "B" if et % 2 == 0 else "A"
        phslot = et % 2
        if et % 2 == 0 and et + 2 < NE:
            for n in ("k", "v", "q"):
                load_eighth(n, (et // 2 + 1) % 2, et + 2)
        proj_mms(et, "k", pk, None, [])
        phi_evac(pk, ph_t[phslot], buf(f"ph_t{phslot}"), 1.0)
        proj_mms(et, "v", pv, None, [])
        pl.emit("vector", [lambda E, pv=pv, phslot=phslot: E.scalar_tensor_tensor(
            out=S_t[:], in0=psum_view(pv, 0, 4), scalar=1.0, in1=ph_t[phslot][:],
            op0=ALU.mult, op1=ALU.mult)],
            reads=banks(pv, 0, 4) + [buf(f"ph_t{phslot}")], writes=[buf("S_t")])
        ct = C_t[et % 2]
        ctu = buf(f"C_t{et % 2}")
        pl.emit("vector", [lambda E, ct=ct: E.tensor_tensor_scan(
            out=ct[:], data0=S_t[:], data1=S_t[:], initial=0.0,
            op0=ALU.add, op1=ALU.bypass)],
            reads=[buf("S_t")], writes=[ctu])
        pl.emit("vector", [lambda E, ct=ct, et=et: E.tensor_copy(
            out=totals[:, et:et + 1], in_=ct[:, T - 1:T])],
            reads=[ctu], writes=[buf("totals")])
        pl.dma("gpsimd", lambda et=et: cloc[et * P:(et + 1) * P, :],
               lambda ct=ct: ct[:], reads=[ctu], writes=[buf("cloc")])
        proj_mms(et, "q", pk, None, [buf(f"ph_t{phslot}")])
        phi_evac(pk, ph_t[phslot], buf(f"ph_t{phslot}"), SCALE)
        pl.dma("gpsimd", lambda et=et: phq[et * P:(et + 1) * P, :],
               lambda phslot=phslot: ph_t[phslot][:],
               reads=[buf(f"ph_t{phslot}")], writes=[buf("phq")])
        quant_pump(5)

    pl.dma("gpsimd", lambda: agi, lambda: totals[:], reads=[buf("totals")],
           writes=[buf("agi")])
    pl.emit("gpsimd", [lambda E: E.collective_compute(
        "AllGather", ALU.bypass, replica_groups=[list(range(8))],
        ins=[agi.opt()], outs=[ago.opt()])], reads=[buf("agi")], writes=[buf("ago")],
        sem="cc")
    pl.dma("gpsimd", lambda: agt.ap().rearrange("p (r c) -> p r c", c=16),
           lambda: ago.rearrange("(r p) c -> p r c", p=P),
           reads=[buf("ago")], writes=[buf("agt")])
    pl.emit("vector", [lambda E: E.tensor_tensor(
        out=maskd.ap().rearrange("p (c r) -> p r c", r=8),
        in0=agt.ap().rearrange("p (r c) -> p r c", c=16),
        in1=maskb.ap().unsqueeze(2).broadcast_to([P, 8, 16]),
        op=ALU.mult)], reads=[buf("agt"), buf("maskb")], writes=[buf("maskd")])
    pl.emit("vector", [lambda E: E.tensor_reduce(
        out=pcol.ap().unsqueeze(2),
        in_=maskd.ap().rearrange("p (c r) -> p c r", r=8),
        axis=AX.X, op=ALU.add)], reads=[buf("maskd")], writes=[buf("pcol")])

    scopeA1.close()
    pl.barrier()

    scopeA2 = ExitStack()
    e2 = scopeA2.enter_context
    cq = e2(nc.sbuf_tensor("cq", [P, NE * 512], BF16))
    pq = e2(nc.sbuf_tensor("pq", [P, NE * 512], BF16))
    ctx = e2(nc.sbuf_tensor("ctx", [P, NE * 512], BF16))
    xres = [e2(nc.sbuf_tensor(f"xres{i}", [P, D], F32)) for i in range(2)]
    ypre = e2(nc.sbuf_tensor("ypre", [P, D], F32))
    y1f_t = [e2(nc.sbuf_tensor(f"y1f_t{i}", [P, D], F32)) for i in range(2)]
    y1b_t = [e2(nc.sbuf_tensor(f"y1b_t{i}", [P, D], BF16)) for i in range(2)]
    st_s = e2(nc.sbuf_tensor("st_s", [P, 1], F32))
    st_q = e2(nc.sbuf_tensor("st_q", [P, 1], F32))
    st_m = e2(nc.sbuf_tensor("st_m", [P, 1], F32))
    st_m2 = e2(nc.sbuf_tensor("st_m2", [P, 1], F32))
    st_v = e2(nc.sbuf_tensor("st_v", [P, 1], F32))

    for ec in range(NE):
        pl.dma("scalar", lambda ec=ec: AR1[:, ec * T:(ec + 1) * T],
               lambda ec=ec: wT["o"][ec * P:(ec + 1) * P, :],
               reads=[buf("wTo")] + [buf(f"ar1_{ec}")],
               writes=[buf(f"ar1_{ec}")])

    for tq in range(4):
        pl.dma("scalar",
               lambda: cq.ap().rearrange("p (ec t) -> p ec t", t=512),
               lambda tq=tq: cloc.rearrange("(ec p) t -> p ec t", p=P)
               [:, :, tq * 512:(tq + 1) * 512],
               reads=[buf("cloc")], writes=[buf("cq")])
        pl.dma("scalar",
               lambda: pq.ap().rearrange("p (ec t) -> p ec t", t=512),
               lambda tq=tq: phq.rearrange("(ec p) t -> p ec t", p=P)
               [:, :, tq * 512:(tq + 1) * 512],
               reads=[buf("phq")], writes=[buf("pq")])
        ops = []
        for ec in range(NE):
            ops.append(lambda E, ec=ec: E.scalar_tensor_tensor(
                out=ctx[:, ec * 512:(ec + 1) * 512],
                in0=cq[:, ec * 512:(ec + 1) * 512],
                scalar=pcol[:, ec:ec + 1],
                in1=pq[:, ec * 512:(ec + 1) * 512],
                op0=ALU.add, op1=ALU.mult))
        pl.emit("vector", ops, reads=[buf("cq"), buf("pq"), buf("pcol")],
                writes=[buf("ctx")])

        for trel in range(4):
            tt = tq * 4 + trel
            pw = "A" if tt % 2 == 0 else "B"
            xr = xres[tt % 2]
            xru = buf(f"xres{tt % 2}")
            pl.dma("scalar", lambda xr=xr: xr[:],
                   lambda tt=tt: x_in[tt * P:(tt + 1) * P, :],
                   writes=[xru])
            mms = []
            for ec in range(NE):
                for dn in range(4):
                    mms.append(lambda E, ec=ec, dn=dn, trel=trel, pw=pw: E.matmul(
                        psum_view(pw, dn, dn + 1),
                        ctx[:, ec * 512 + trel * P: ec * 512 + (trel + 1) * P],
                        AR1[:, ec * T + dn * 512: ec * T + (dn + 1) * 512],
                        start=(ec == 0), stop=False))
            for dn in range(4):
                mms.append(lambda E, dn=dn, pw=pw: E.matmul(
                    psum_view(pw, dn, dn + 1),
                    ones_row[:, 0:P].bitcast(BF16) if False else ones_row[:, 0:P],
                    brow["o"][:, dn * 512:(dn + 1) * 512],
                    start=False, stop=True))
            pl.emit("tensor", mms,
                    reads=[buf("ctx"), buf("browo"), buf("consts")] +
                          [buf(f"ar1_{ec}") for ec in range(NE)],
                    writes=banks(pw, 0, 4), sem="pe")
            pb = banks(pw, 0, 4)
            pl.emit("vector", [lambda E, pw=pw, xr=xr: E.scalar_tensor_tensor(
                out=ypre[:], in0=psum_view(pw, 0, 4), scalar=1.0, in1=xr[:],
                op0=ALU.mult, op1=ALU.add, accum_out=st_s[:])],
                reads=pb + [xru], writes=[buf("ypre"), buf("st_s")])
            pl.emit("scalar", [lambda E, xr=xr: E.activation(
                out=xr[:], in_=ypre[:], func=AF.Square, accum_out=st_q[:])],
                reads=[buf("ypre")], writes=[xru, buf("st_q")])
            pl.emit("vector", [lambda E: E.tensor_scalar(
                out=st_m[:], in0=st_s[:], scalar1=INV_D, scalar2=None,
                op0=ALU.mult)], reads=[buf("st_s")], writes=[buf("st_m")])
            pl.emit("scalar", [lambda E: E.activation(
                out=st_m2[:], in_=st_m[:], func=AF.Square)],
                reads=[buf("st_m")], writes=[buf("st_m2")])
            pl.emit("vector", [lambda E: E.scalar_tensor_tensor(
                out=st_v[:], in0=st_q[:], scalar=INV_D, in1=st_m2[:],
                op0=ALU.mult, op1=ALU.subtract)],
                reads=[buf("st_q"), buf("st_m2")], writes=[buf("st_v")])
            pl.emit("scalar", [lambda E: E.activation(
                out=st_v[:], in_=st_v[:], func=AF.Ln, bias=c_eln[:])],
                reads=[buf("st_v"), buf("consts")], writes=[buf("st_v")])
            pl.emit("scalar", [lambda E: E.activation(
                out=st_v[:], in_=st_v[:], func=AF.Exp, scale=-0.5)],
                reads=[buf("st_v")], writes=[buf("st_v")])
            y1t = y1f_t[tt % 2]
            y1tu = buf(f"y1f_t{tt % 2}")
            pl.emit("vector", [lambda E, y1t=y1t: E.tensor_scalar(
                out=y1t[:], in0=ypre[:], scalar1=st_m[:], scalar2=st_v[:],
                op0=ALU.subtract, op1=ALU.mult)],
                reads=[buf("ypre"), buf("st_m"), buf("st_v")], writes=[y1tu])
            y1bt = y1b_t[tt % 2]
            y1btu = buf(f"y1b_t{tt % 2}")
            pl.emit("vector", [lambda E, y1t=y1t, y1bt=y1bt: E.tensor_copy(
                out=y1bt[:], in_=y1t[:])], reads=[y1tu], writes=[y1btu])
            pl.dma("gpsimd", lambda tt=tt: y1f[tt * P:(tt + 1) * P, :],
                   lambda y1t=y1t: y1t[:], reads=[y1tu], writes=[buf("y1f")])
            pl.dma("gpsimd", lambda tt=tt: y1b[tt * P:(tt + 1) * P, :],
                   lambda y1bt=y1bt: y1bt[:], reads=[y1btu], writes=[buf("y1b")])
            quant_pump(3)

    scopeA2.close()
    scopeA.close()

    quant_flush()
    pretr_pump()
    assert pretr_ix[0] == len(pretr_todo)

    for dc in range(ND):
        bb = bounce[dc % 2]
        bbuf = buf(f"bounce{dc % 2}")
        pl.dma("sync", lambda bb=bb: bb[:],
               lambda dc=dc: y1b[:, dc * P:(dc + 1) * P],
               reads=[buf("y1b")], writes=[bbuf], transpose=True)
        pl.dma("scalar", lambda dc=dc: y1T[dc * P:(dc + 1) * P, :],
               lambda bb=bb: bb[:], reads=[bbuf], writes=[buf("y1T")])

    scopeB = ExitStack()
    eB = scopeB.enter_context
    AR2 = eB(nc.sbuf_tensor("AR2", [P, NH * 512], BF16))
    y1g = [eB(nc.sbuf_tensor("y1g0", [P, ND * 512], BF16))]
    wfcs = [eB(nc.sbuf_tensor(f"wfcs{i}", [P, ND * 256], BF16)) for i in range(2)]
    wprs = [eB(nc.sbuf_tensor(f"wprs{i}", [P, 8 * 512], BF16)) for i in range(2)]
    y1p = [eB(nc.sbuf_tensor(f"y1p{i}", [P, 512], F32)) for i in range(2)]
    out_t = [eB(nc.sbuf_tensor(f"out_t{i}", [P, D], F32)) for i in range(4)]
    rfc = eB(nc.sbuf_tensor("rfc", [P, 512], F32))
    gsqs = [eB(nc.sbuf_tensor(f"gsqs{i}", [P, 512], BF16)) for i in range(2)]
    sqp = eB(nc.sbuf_tensor("sqp", [P, 512], F32))
    rstd_row = eB(nc.sbuf_tensor("rstd_row", [1, 512], F32))
    rstdc = eB(nc.sbuf_tensor("rstdc", [P, 4], F32))
    acc_s = eB(nc.sbuf_tensor("acc_s", [P, 16], F32))
    acc_q = eB(nc.sbuf_tensor("acc_q", [P, 16], F32))
    bst_s = eB(nc.sbuf_tensor("bst_s", [P, 1], F32))
    bst_q = eB(nc.sbuf_tensor("bst_q", [P, 1], F32))
    bst_m = eB(nc.sbuf_tensor("bst_m", [P, 1], F32))
    bst_m2 = eB(nc.sbuf_tensor("bst_m2", [P, 1], F32))
    bst_v = eB(nc.sbuf_tensor("bst_v", [P, 1], F32))

    pl.barrier()

    for grp in range(4):
        gslot = 0
        pl.dma("sync",
               lambda gslot=gslot: y1g[gslot].ap().rearrange("p (dc t) -> p dc t", t=512),
               lambda grp=grp: y1T.rearrange("(dc p) t -> p dc t", p=P)
               [:, :, grp * 512:(grp + 1) * 512],
               reads=[buf("y1T")], writes=[buf(f"y1g{gslot}")])
        pend_ones = []
        for hpair in range(NH // 2):
            wslot = hpair % 2
            pl.dma("sync",
                   lambda wslot=wslot: wfcs[wslot].ap()
                   .rearrange("p (dc h) -> p dc h", h=256),
                   lambda hpair=hpair: wfcT.rearrange("(dc p) h -> p dc h", p=P)
                   [:, :, hpair * 256:(hpair + 1) * 256],
                   reads=[buf("wfcT")], writes=[buf(f"wfcs{wslot}")])
            for hrel in range(2):
                hc = hpair * 2 + hrel
                bank = hc % 4
                mms = []
                for dc in range(ND):
                    mms.append(lambda E, wslot=wslot, dc=dc, hrel=hrel, gslot=gslot,
                               bank=bank: E.matmul(
                                   psum_view("A", bank, bank + 1),
                                   wfcs[wslot][:, dc * 256 + hrel * P:
                                               dc * 256 + (hrel + 1) * P],
                                   y1g[gslot][:, dc * 512:(dc + 1) * 512],
                                   start=(dc == 0), stop=(dc == ND - 1)))
                pl.emit("tensor", mms,
                        reads=[buf(f"wfcs{wslot}"), buf(f"y1g{gslot}")],
                        writes=banks("A", bank, bank + 1), sem="pe")
                pbk = banks("A", bank, bank + 1)
                pl.emit("scalar", [lambda E, bank=bank: E.activation(
                    out=rfc[:], in_=psum_view("A", bank, bank + 1), func=AF.Relu)],
                    reads=pbk, writes=[buf("rfc")])
                pl.emit("scalar", [lambda E, hc=hc: E.activation(
                    out=AR2[:, hc * 512:(hc + 1) * 512], in_=rfc[:],
                    func=AF.Square)], reads=[buf("rfc")], writes=[buf(f"g{hc}")])
                pl.emit("scalar", [lambda E, hc=hc: E.activation(
                    out=gsqs[hc % 2][:], in_=AR2[:, hc * 512:(hc + 1) * 512],
                    func=AF.Square)], reads=[buf(f"g{hc}")],
                    writes=[buf(f"gsqs{hc % 2}")])
                pend_ones.append(hc)
                if len(pend_ones) > 2:
                    h0 = pend_ones.pop(0)
                    pl.emit("tensor", [lambda E, h0=h0: E.matmul(
                        psum_view("B", 0, 1), ones128[:], gsqs[h0 % 2][:],
                        start=(h0 == 0), stop=(h0 == NH - 1))],
                        reads=[buf(f"gsqs{h0 % 2}"), buf("consts")],
                        writes=banks("B", 0, 1), sem="pe")
        while pend_ones:
            h0 = pend_ones.pop(0)
            pl.emit("tensor", [lambda E, h0=h0: E.matmul(
                psum_view("B", 0, 1), ones128[:], gsqs[h0 % 2][:],
                start=(h0 == 0), stop=(h0 == NH - 1))],
                reads=[buf(f"gsqs{h0 % 2}"), buf("consts")],
                writes=banks("B", 0, 1), sem="pe")
        pl.emit("scalar", [lambda E: E.activation(
            out=rstd_row[:], in_=psum_view("B", 0, 1)[0:1, :],
            func=AF.Ln, scale=INV_H, bias=c_erm[0:1, :])],
            reads=banks("B", 0, 1) + [buf("consts")], writes=[buf("rstd_row")])
        pl.emit("scalar", [lambda E: E.activation(
            out=rstd_row[:], in_=rstd_row[:], func=AF.Exp, scale=-0.5)],
            reads=[buf("rstd_row")], writes=[buf("rstd_row")])
        tmms = []
        for trel in range(4):
            tmms.append(lambda E, trel=trel: E.matmul(
                pB[:, 512 + trel:512 + trel + 1],
                rstd_row[:, trel * P:(trel + 1) * P], ones1f[:],
                start=True, stop=True))
        pl.emit("tensor", tmms, reads=[buf("rstd_row"), buf("consts")],
                writes=banks("B", 1, 2), sem="pe")
        pl.emit("vector", [lambda E: E.tensor_copy(
            out=rstdc[:], in_=pB[:, 512:516])],
            reads=banks("B", 1, 2), writes=[buf("rstdc")])

        for dq in range(4):
            pw = "A" if dq % 2 == 0 else "B"
            for hb in range(8):
                wslot = hb % 2
                pl.dma("sync",
                       lambda wslot=wslot: wprs[wslot].ap()
                       .rearrange("p (hc d) -> p hc d", d=512),
                       lambda hb=hb, dq=dq: wprT.rearrange("(hc p) d -> p hc d", p=P)
                       [:, hb * 8:(hb + 1) * 8, dq * 512:(dq + 1) * 512],
                       reads=[buf("wprT")], writes=[buf(f"wprs{wslot}")])
                mms = []
                for hrel in range(8):
                    hc = hb * 8 + hrel
                    for trel in range(4):
                        mms.append(lambda E, hc=hc, hrel=hrel, trel=trel,
                                   wslot=wslot, pw=pw: E.matmul(
                                       psum_view(pw, trel, trel + 1),
                                       AR2[:, hc * 512 + trel * P:
                                           hc * 512 + (trel + 1) * P],
                                       wprs[wslot][:, hrel * 512:(hrel + 1) * 512],
                                       start=(hc == 0), stop=(hc == NH - 1)))
                pl.emit("tensor", mms,
                        reads=[buf(f"wprs{wslot}")] +
                              [buf(f"g{hb * 8 + i}") for i in range(8)],
                        writes=banks(pw, 0, 4), sem="pe")
            for trel in range(4):
                tt = grp * 4 + trel
                ys = y1p[trel % 2]
                ysu = buf(f"y1p{trel % 2}")
                pl.dma("scalar", lambda ys=ys: ys[:],
                       lambda tt=tt, dq=dq: y1f[tt * P:(tt + 1) * P,
                                                dq * 512:(dq + 1) * 512],
                       reads=[buf("y1f")], writes=[ysu])
                pbk = banks(pw, trel, trel + 1)
                oid = trel
                col = trel * 4 + dq
                pl.emit("vector", [lambda E, pw=pw, trel=trel, ys=ys, oid=oid,
                                   dq=dq, col=col: E.scalar_tensor_tensor(
                    out=out_t[oid][:, dq * 512:(dq + 1) * 512],
                    in0=psum_view(pw, trel, trel + 1),
                    scalar=rstdc[:, trel:trel + 1], in1=ys[:],
                    op0=ALU.mult, op1=ALU.add,
                    accum_out=acc_s[:, col:col + 1])],
                    reads=pbk + [ysu, buf("rstdc")],
                    writes=[buf(f"outp{oid}_{dq}"), buf("acc_s")])
                pl.emit("scalar", [lambda E, oid=oid, dq=dq, col=col: E.activation(
                    out=sqp[:], in_=out_t[oid][:, dq * 512:(dq + 1) * 512],
                    func=AF.Square, accum_out=acc_q[:, col:col + 1])],
                    reads=[buf(f"outp{oid}_{dq}")],
                    writes=[buf("sqp"), buf("acc_q")])
        for trel in range(4):
            tt = grp * 4 + trel
            oid = trel
            opieces = [buf(f"outp{oid}_{dq}") for dq in range(4)]
            pl.emit("vector", [lambda E, trel=trel: E.tensor_reduce(
                out=bst_s.ap().unsqueeze(2),
                in_=acc_s.ap()[:, trel * 4:(trel + 1) * 4].unsqueeze(1),
                axis=AX.X, op=ALU.add)],
                reads=[buf("acc_s")], writes=[buf("bst_s")])
            pl.emit("vector", [lambda E, trel=trel: E.tensor_reduce(
                out=bst_q.ap().unsqueeze(2),
                in_=acc_q.ap()[:, trel * 4:(trel + 1) * 4].unsqueeze(1),
                axis=AX.X, op=ALU.add)],
                reads=[buf("acc_q")], writes=[buf("bst_q")])
            pl.emit("vector", [lambda E: E.tensor_scalar(
                out=bst_m[:], in0=bst_s[:], scalar1=INV_D, scalar2=None,
                op0=ALU.mult)], reads=[buf("bst_s")], writes=[buf("bst_m")])
            pl.emit("scalar", [lambda E: E.activation(
                out=bst_m2[:], in_=bst_m[:], func=AF.Square)],
                reads=[buf("bst_m")], writes=[buf("bst_m2")])
            pl.emit("vector", [lambda E: E.scalar_tensor_tensor(
                out=bst_v[:], in0=bst_q[:], scalar=INV_D, in1=bst_m2[:],
                op0=ALU.mult, op1=ALU.subtract)],
                reads=[buf("bst_q"), buf("bst_m2")], writes=[buf("bst_v")])
            pl.emit("scalar", [lambda E: E.activation(
                out=bst_v[:], in_=bst_v[:], func=AF.Ln, bias=c_eln[:])],
                reads=[buf("bst_v"), buf("consts")], writes=[buf("bst_v")])
            pl.emit("scalar", [lambda E: E.activation(
                out=bst_v[:], in_=bst_v[:], func=AF.Exp, scale=-0.5)],
                reads=[buf("bst_v")], writes=[buf("bst_v")])
            pl.emit("vector", [lambda E, oid=oid: E.tensor_scalar(
                out=out_t[oid][:], in0=out_t[oid][:], scalar1=bst_m[:],
                scalar2=bst_v[:], op0=ALU.subtract, op1=ALU.mult)],
                reads=opieces + [buf("bst_m"), buf("bst_v")],
                writes=opieces)
            pl.dma("scalar", lambda tt=tt: out_hbm[tt * P:(tt + 1) * P, :],
                   lambda oid=oid: out_t[oid][:],
                   reads=opieces, writes=[buf("outdma")])

    scopeB.close()

    final_dc = pl.cnt.get("dc", 0)

    with nc.Block() as block:

        def run(E, q):
            for item in q:
                if item[0] == "w":
                    E.wait_ge(sems[item[1]], item[2])
                elif item[0] == "p":
                    item[1](E)
                else:
                    item[1](E).then_inc(sems[item[2]], item[3])

        @block.sync
        def _(E):
            run(E, pl.q["sync"])
            E.wait_ge(sems["dc"], final_dc)

        @block.scalar
        def _(E):
            run(E, pl.q["scalar"])

        @block.vector
        def _(E):
            run(E, pl.q["vector"])

        @block.tensor
        def _(E):
            run(E, pl.q["tensor"])

        @block.gpsimd
        def _(E):
            run(E, pl.q["gpsimd"])

    st.close()
    return nc


_NC_CACHE = {}


def _get_nc():
    if "nc" not in _NC_CACHE:
        _NC_CACHE["nc"] = build_nc()
    return _NC_CACHE["nc"]


def kernel(L0_attn_x, L0_attn_Wq, L0_attn_bq, L0_attn_Wk, L0_attn_bk,
           L0_attn_Wv, L0_attn_bv, L0_attn_Wo, L0_attn_bo,
           L0_ffn_W_fc, L0_ffn_W_proj):
    x = np.ascontiguousarray(np.asarray(L0_attn_x, dtype=np.float32))
    ws = {n: np.ascontiguousarray(np.asarray(w, dtype=np.float32))
          for n, w in (("q", L0_attn_Wq), ("k", L0_attn_Wk),
                       ("v", L0_attn_Wv), ("o", L0_attn_Wo))}
    bs = {n: np.ascontiguousarray(np.asarray(b, dtype=np.float32).reshape(1, D))
          for n, b in (("q", L0_attn_bq), ("k", L0_attn_bk),
                       ("v", L0_attn_bv), ("o", L0_attn_bo))}
    wfc = np.ascontiguousarray(np.asarray(L0_ffn_W_fc, dtype=np.float32))
    wpr = np.ascontiguousarray(np.asarray(L0_ffn_W_proj, dtype=np.float32))

    in_maps = []
    for c in range(8):
        b, j = c // 4, c % 4
        m = np.zeros((P, 8), dtype=np.float32)
        for r in range(8):
            if r // 4 == b and (r % 4) < j:
                m[:, r] = 1.0
        im = {"x": np.ascontiguousarray(x[b, j * T:(j + 1) * T, :]),
              "mask": m, "wfc": wfc, "wpr": wpr}
        for n in ("q", "k", "v", "o"):
            im["w" + n] = ws[n]
            im["b" + n] = bs[n]
        in_maps.append(im)

    nc = _get_nc()
    res = run_bass_kernel_spmd(nc, in_maps, core_ids=list(range(8)))
    out = np.empty((2, 8192, D), dtype=np.float32)
    for c in range(8):
        b, j = c // 4, c % 4
        out[b, j * T:(j + 1) * T, :] = np.asarray(res.results[c]["out"],
                                                  dtype=np.float32)
    return out


# version 06
# speedup vs baseline: 1.3167x; 1.3167x over previous
import numpy as np
import ml_dtypes
from contextlib import ExitStack

import concourse.bass as bass
import concourse.mybir as mybir
from concourse.bass_utils import run_bass_kernel_spmd

F32 = mybir.dt.float32
BF16 = mybir.dt.bfloat16
AF = mybir.ActivationFunctionType
ALU = mybir.AluOpType
AX = mybir.AxisListType
bfdt = ml_dtypes.bfloat16

P = 128
D = 2048
T = 2048
HID = 8192
ND = D // P
NE = D // P
NT = T // P
NH = HID // P
SCALE = float(np.float32(128 ** -0.5))
MAGIC = float(np.float32(1.5 * 2 ** 23))
EPS_LN = 1e-5
EPS_RMS = float(np.float32(np.finfo(np.float32).eps))
INV_D = 1.0 / 2048.0
INV_H = 1.0 / 8192.0
INV_G = 1.0 / 64.0

ENGINES = ("sync", "scalar", "vector", "tensor", "gpsimd")
QSEM = {"sync": "ds", "scalar": "dc", "gpsimd": "dg"}
ESEM = {"tensor": "pe", "vector": "dv", "scalar": "ac", "gpsimd": "po"}


class Buf:
    __slots__ = ("w", "readers")

    def __init__(self):
        self.w = None
        self.readers = []


class Plan:
    def __init__(self):
        self.q = {e: [] for e in ENGINES}
        self.cnt = {}
        self.sat = {e: {} for e in ENGINES}

    def _wait(self, eng, sem, val):
        if val <= 0 or self.sat[eng].get(sem, 0) >= val:
            return
        self.sat[eng][sem] = val
        self.q[eng].append(("w", sem, val))

    def emit(self, eng, builders, reads=(), writes=(), sem=None, inc=1):
        need = {}
        for b in reads:
            if b.w:
                s, v = b.w
                need[s] = max(need.get(s, 0), v)
        for b in writes:
            if b.w:
                s, v = b.w
                need[s] = max(need.get(s, 0), v)
            for s, v in b.readers:
                need[s] = max(need.get(s, 0), v)
        for s, v in need.items():
            self._wait(eng, s, v)
        if sem is None:
            sem = QSEM.get(eng) if False else ESEM.get(eng)
        self.cnt[sem] = self.cnt.get(sem, 0) + inc
        tok = (sem, self.cnt[sem])
        if not isinstance(builders, (list, tuple)):
            builders = [builders]
        for i, b in enumerate(builders):
            if i == len(builders) - 1:
                self.q[eng].append(("i", b, sem, inc))
            else:
                self.q[eng].append(("p", b))
        for b in writes:
            b.w = tok
            b.readers = []
        for b in reads:
            b.readers.append(tok)
        return tok

    def dma(self, queue, out_ap_f, in_ap_f, reads=(), writes=(), **kw):
        sem = QSEM[queue]
        return self.emit(
            queue,
            [lambda E, o=out_ap_f, i=in_ap_f, kw=kw: E.dma_start(out=o(), in_=i(), **kw)],
            reads=reads, writes=writes, sem=sem, inc=16,
        )

    def barrier(self):
        snap = dict(self.cnt)
        for eng in ENGINES:
            for s, v in snap.items():
                self._wait(eng, s, v)


def build_nc():
    nc = bass.Bass("TRN2", target_bir_lowering=False, detect_race_conditions=False)
    pl = Plan()

    x_in = nc.dram_tensor("x", [T, D], F32, kind="ExternalInput").ap()
    w_in = {n: nc.dram_tensor("w" + n, [D // 8, D], F32, kind="ExternalInput").ap()
            for n in ("q", "k", "v", "o")}
    b_in = {n: nc.dram_tensor("b" + n, [1, D], F32, kind="ExternalInput").ap()
            for n in ("q", "k", "v", "o")}
    wfcq_in = nc.dram_tensor("wfcq", [HID // 8, D], F32, kind="ExternalInput").ap()
    wprq_in = nc.dram_tensor("wprq", [D // 8, HID], F32, kind="ExternalInput").ap()
    mask_in = nc.dram_tensor("mask", [P, 8], F32, kind="ExternalInput").ap()
    out_hbm = nc.dram_tensor("out", [T, D], F32, kind="ExternalOutput").ap()

    xb = nc.dram_tensor("xb", [T, D], BF16).ap()
    wp = {n: nc.dram_tensor("wp" + n, [D // 8, D], BF16).ap() for n in ("q", "k", "v", "o")}
    wb = {n: nc.dram_tensor("wb" + n, [D, D], BF16).ap() for n in ("q", "k", "v", "o")}
    wT = {n: nc.dram_tensor("wT" + n, [D, D], BF16).ap() for n in ("q", "k", "v", "o")}
    w1p = nc.dram_tensor("w1p", [HID // 8, D], BF16).ap()
    w2p = nc.dram_tensor("w2p", [D // 8, HID], BF16).ap()
    w1n = nc.dram_tensor("w1n", [HID, D], BF16).ap()
    w2n = nc.dram_tensor("w2n", [D, HID], BF16).ap()
    wfcT = nc.dram_tensor("wfcT", [D, HID], BF16).ap()
    wprT = nc.dram_tensor("wprT", [HID, D], BF16).ap()
    cloc = nc.dram_tensor("cloc", [D, T], BF16).ap()
    phq = nc.dram_tensor("phq", [D, T], BF16).ap()
    y1f = nc.dram_tensor("y1f", [T, D], F32).ap()
    y1b = nc.dram_tensor("y1b", [T, D], BF16).ap()
    y1T = nc.dram_tensor("y1T", [D, T], BF16).ap()
    agi = nc.dram_tensor("agi", [P, 16], F32).ap()
    ago = nc.dram_tensor("ago", [8 * P, 16], F32).ap()

    st = ExitStack()
    e = st.enter_context

    sems = {}
    for s in ("pe", "dv", "ac", "po", "cc", "ds", "dc", "dg"):
        sems[s] = e(nc.semaphore(s))

    bounce = [e(nc.sbuf_tensor(f"bounce{i}", [P, D], BF16)) for i in range(2)]
    qw_wb = [e(nc.sbuf_tensor(f"qwb{i}", [P, D], BF16)) for i in range(2)]
    qw_t = [e(nc.sbuf_tensor(f"qwt{i}", [P, D], BF16)) for i in range(2)]
    qgs = [e(nc.sbuf_tensor(f"qgs{i}", [P, 32], F32)) for i in range(2)]
    qsc = [e(nc.sbuf_tensor(f"qsc{i}", [P, 32], BF16)) for i in range(2)]
    qrc = [e(nc.sbuf_tensor(f"qrc{i}", [P, 32], F32)) for i in range(2)]
    totals = e(nc.sbuf_tensor("totals", [P, 16], F32))
    agt = e(nc.sbuf_tensor("agt", [P, 8 * 16], F32))
    maskb = e(nc.sbuf_tensor("maskb", [P, 8], F32))
    maskd = e(nc.sbuf_tensor("maskd", [P, 16 * 8], F32))
    pcol = e(nc.sbuf_tensor("pcol", [P, 16], F32))
    ones_row = e(nc.sbuf_tensor("ones_row", [1, 512], BF16))
    ones128 = e(nc.sbuf_tensor("ones128", [P, P], BF16))
    ones1f = e(nc.sbuf_tensor("ones1f", [1, 1], F32))
    c_eln = e(nc.sbuf_tensor("c_eln", [P, 1], F32))
    c_erm = e(nc.sbuf_tensor("c_erm", [P, 1], F32))

    pA = e(nc.psum_tensor("pA", [P, D], F32))
    pB = e(nc.psum_tensor("pB", [P, D], F32))
    pbank = [Buf() for _ in range(8)]

    def psum_view(which, lo, hi):
        t_ = pA if which == "A" else pB
        return t_[:, lo * 512:hi * 512]

    def banks(which, lo, hi):
        off = 0 if which == "A" else 4
        return [pbank[off + i] for i in range(lo, hi)]

    B_ = {}

    def buf(name):
        if name not in B_:
            B_[name] = Buf()
        return B_[name]

    pl.emit("vector", [
        lambda E: E.memset(ones_row[:], 1.0),
        lambda E: E.memset(ones128[:], 1.0),
        lambda E: E.memset(ones1f[:], 1.0),
        lambda E: E.memset(c_eln[:], EPS_LN),
        lambda E: E.memset(c_erm[:], EPS_RMS),
    ], writes=[buf("consts")], sem="dv")
    pl.dma("scalar", lambda: maskb[:], lambda: mask_in, writes=[buf("maskb")])

    brow = {n: e(nc.sbuf_tensor(f"brow{n}", [1, D], BF16)) for n in ("q", "k", "v", "o")}
    for n in ("q", "k", "v", "o"):
        pl.dma("gpsimd", lambda n=n: brow[n][:], lambda n=n: b_in[n], writes=[buf("brow" + n)])

    for n in ("k", "v", "q", "o"):
        pl.dma("gpsimd", lambda n=n: wp[n], lambda n=n: w_in[n],
               writes=[buf(f"wp{n}")])
        pl.emit("gpsimd", [lambda E, n=n: E.collective_compute(
            "AllGather", ALU.bypass, replica_groups=[list(range(8))],
            ins=[wp[n].opt()], outs=[wb[n].opt()])],
            reads=[buf(f"wp{n}")], writes=[buf(f"wb{n}")], sem="cc")
        if n == "v":
            for ci in range(4):
                r0, r1 = ci * (T // 4), (ci + 1) * (T // 4)
                pl.dma("gpsimd", lambda r0=r0, r1=r1: xb[r0:r1, :],
                       lambda r0=r0, r1=r1: x_in[r0:r1, :],
                       writes=[buf("xb")])

    scopeA = ExitStack()
    eA = scopeA.enter_context
    AR1 = eA(nc.sbuf_tensor("AR1", [P, ND * T], BF16))

    def pretranspose_weight(n):
        for dc in range(ND):
            bb = bounce[dc % 2]
            bbuf = buf(f"bounce{dc % 2}")
            pl.dma("sync",
                   lambda bb=bb: bb[:],
                   lambda n=n, dc=dc: wb[n][:, dc * P:(dc + 1) * P],
                   reads=[buf(f"wb{n}")], writes=[bbuf], transpose=True)
            pl.dma("scalar",
                   lambda n=n, dc=dc: wT[n][dc * P:(dc + 1) * P, :],
                   lambda bb=bb: bb[:],
                   reads=[bbuf], writes=[buf(f"wT{n}")])

    pretranspose_weight("k")
    pretranspose_weight("v")
    for dc in range(ND):
        pl.dma("sync",
               lambda dc=dc: AR1[:, dc * T:(dc + 1) * T],
               lambda dc=dc: xb[:, dc * P:(dc + 1) * P],
               reads=[buf("xb")], writes=[buf(f"ar1_{dc}")], transpose=True)
    pretranspose_weight("q")
    pretranspose_weight("o")

    qcount = [0]

    def quant_stage1(src_ap_f, i):
        wbb, tb = qw_wb[i], qw_t[i]
        gs, sc, rc = qgs[i], qsc[i], qrc[i]
        wbu, tbu = buf(f"qwb{i}"), buf(f"qwt{i}")
        gsu, scu, rcu = buf(f"qgs{i}"), buf(f"qsc{i}"), buf(f"qrc{i}")
        pl.dma("gpsimd", lambda wbb=wbb: wbb[:], src_ap_f, writes=[wbu])
        pl.emit("vector", [lambda E, wbb=wbb, gs=gs: E.tensor_reduce(
            out=gs.ap().unsqueeze(2),
            in_=wbb.ap().rearrange("p (g j) -> p g j", j=64),
            axis=AX.X, op=ALU.add, apply_absolute_value=True)],
            reads=[wbu], writes=[gsu])
        pl.emit("vector", [lambda E, gs=gs, sc=sc: E.tensor_scalar(
            out=sc[:], in0=gs[:], scalar1=INV_G, scalar2=1e-8,
            op0=ALU.mult, op1=ALU.max)], reads=[gsu], writes=[scu])
        pl.emit("vector", [lambda E, sc=sc, rc=rc: E.reciprocal(out=rc[:], in_=sc[:])],
                reads=[scu], writes=[rcu])
        pl.emit("gpsimd", [lambda E, wbb=wbb, rc=rc, tb=tb: E.tensor_tensor(
            out=tb.ap().rearrange("p (g j) -> p g j", j=64),
            in0=wbb.ap().rearrange("p (g j) -> p g j", j=64),
            in1=rc.ap().unsqueeze(2).broadcast_to([P, 32, 64]),
            op=ALU.mult)], reads=[wbu, rcu], writes=[tbu], sem="po")

    def quant_stage2(dst_ap_f, tag, i):
        wbb, tb = qw_wb[i], qw_t[i]
        sc = qsc[i]
        wbu, tbu, scu = buf(f"qwb{i}"), buf(f"qwt{i}"), buf(f"qsc{i}")
        pl.emit("vector", [lambda E, tb=tb: E.tensor_scalar(
            out=tb[:], in0=tb[:], scalar1=MAGIC, scalar2=MAGIC,
            op0=ALU.add, op1=ALU.subtract)], reads=[tbu], writes=[tbu])
        pl.emit("vector", [lambda E, tb=tb: E.tensor_scalar(
            out=tb[:], in0=tb[:], scalar1=1.0, scalar2=-1.0,
            op0=ALU.min, op1=ALU.max)], reads=[tbu], writes=[tbu])
        pl.emit("vector", [lambda E, tb=tb, sc=sc, wbb=wbb: E.tensor_tensor(
            out=wbb.ap().rearrange("p (g j) -> p g j", j=64),
            in0=tb.ap().rearrange("p (g j) -> p g j", j=64),
            in1=sc.ap().unsqueeze(2).broadcast_to([P, 32, 64]),
            op=ALU.mult)], reads=[tbu, scu], writes=[wbu])
        pl.dma("scalar", dst_ap_f, lambda wbb=wbb: wbb[:],
               reads=[wbu], writes=[buf(tag)])

    fc_blocks = [(lambda r=r: wfcq_in[r * P:(r + 1) * P, :],
                  lambda r=r: w1p[r * P:(r + 1) * P, :]) for r in range(HID // 8 // P)]
    pr_blocks = []
    for r in range(D // 8 // P):
        for hq in range(4):
            pr_blocks.append(
                (lambda r=r, hq=hq: wprq_in[r * P:(r + 1) * P, hq * D:(hq + 1) * D],
                 lambda r=r, hq=hq: w2p[r * P:(r + 1) * P, hq * D:(hq + 1) * D]))
    quant_todo = [(s, d, "w1p") for (s, d) in fc_blocks] + \
                 [(s, d, "w2p") for (s, d) in pr_blocks]
    quant_s1 = [0]
    quant_s2 = [0]

    def quant_pump(k):
        pretr_pump(6)
        for _ in range(k):
            if quant_s1[0] < len(quant_todo):
                i = quant_s1[0]
                quant_stage1(quant_todo[i][0], i % 2)
                quant_s1[0] += 1
            if quant_s2[0] < quant_s1[0] - 1 or (
                    quant_s1[0] == len(quant_todo) and quant_s2[0] < quant_s1[0]):
                i = quant_s2[0]
                quant_stage2(quant_todo[i][1], quant_todo[i][2], i % 2)
                quant_s2[0] += 1
            pretr_pump()

    def quant_flush():
        while quant_s1[0] < len(quant_todo) or quant_s2[0] < len(quant_todo):
            if quant_s1[0] < len(quant_todo):
                i = quant_s1[0]
                quant_stage1(quant_todo[i][0], i % 2)
                quant_s1[0] += 1
            if quant_s2[0] < quant_s1[0]:
                i = quant_s2[0]
                quant_stage2(quant_todo[i][1], quant_todo[i][2], i % 2)
                quant_s2[0] += 1
            pretr_pump()

    pretr_todo = []
    for hq in range(4):
        for dc in range(ND):
            pretr_todo.append((
                lambda dc=dc, hq=hq: w1n[hq * D:(hq + 1) * D, dc * P:(dc + 1) * P],
                lambda dc=dc, hq=hq: wfcT[dc * P:(dc + 1) * P, hq * D:(hq + 1) * D],
                "w1n", "wfcT"))
    for hc in range(NH):
        pretr_todo.append((
            lambda hc=hc: w2n[:, hc * P:(hc + 1) * P],
            lambda hc=hc: wprT[hc * P:(hc + 1) * P, :],
            "w2n", "wprT"))
    pretr_ix = [0]
    pretr_n = [0]
    ag_done = [False]

    def pretr_pump(k=999):
        while ag_done[0] and k > 0 and pretr_ix[0] < len(pretr_todo):
            inf, outf, stag, dtag = pretr_todo[pretr_ix[0]]
            pretr_ix[0] += 1
            k -= 1
            bb = bounce[pretr_n[0] % 2]
            bbuf = buf(f"bounce{pretr_n[0] % 2}")
            pretr_n[0] += 1
            pl.dma("sync", lambda bb=bb: bb[:], inf,
                   reads=[buf(stag)], writes=[bbuf], transpose=True)
            pl.dma("sync", outf, lambda bb=bb: bb[:],
                   reads=[bbuf], writes=[buf(dtag)])

    scopeA1 = ExitStack()
    e1 = scopeA1.enter_context
    wstream = {n: e1(nc.sbuf_tensor(f"wst{n}", [P, 2 * ND * 256], BF16))
               for n in ("k", "v", "q")}
    r_t = e1(nc.sbuf_tensor("r_t", [P, T], F32))
    m_t = e1(nc.sbuf_tensor("m_t", [P, T], F32))
    S_t = e1(nc.sbuf_tensor("S_t", [P, T], BF16))
    C_t = [e1(nc.sbuf_tensor(f"C_t{i}", [P, T], BF16)) for i in range(2)]
    ph_t = [e1(nc.sbuf_tensor(f"ph_t{i}", [P, T], BF16)) for i in range(2)]

    def wst_view(n, slot, dc, e0, width):
        base = slot * ND * 256 + dc * 256
        return wstream[n][:, base + e0: base + e0 + width]

    def load_eighth(n, slot, et0):
        pl.dma("scalar",
               lambda n=n, slot=slot: wstream[n][:, slot * ND * 256:(slot + 1) * ND * 256]
               .rearrange("p (dc e) -> p dc e", e=256),
               lambda n=n, et0=et0: wT[n].rearrange("(dc p) e -> p dc e", p=P)
               [:, :, et0 * P:(et0 + 2) * P],
               reads=[buf(f"wT{n}")], writes=[buf(f"wst{n}{slot}")])

    for n in ("k", "v", "q"):
        load_eighth(n, 0, 0)

    def proj_mms(et, n, pwhich, phi_scale, extra_reads):
        slot = (et // 2) % 2
        erel = (et % 2) * P
        wsbuf = buf(f"wst{n}{slot}")
        mms = []
        for dc in range(ND):
            for tn in range(4):
                mms.append(lambda E, n=n, slot=slot, dc=dc, erel=erel, tn=tn,
                           pw=pwhich: E.matmul(
                               psum_view(pw, tn, tn + 1),
                               wst_view(n, slot, dc, erel, P),
                               AR1[:, dc * T + tn * 512: dc * T + (tn + 1) * 512],
                               start=(dc == 0), stop=False))
        for tn in range(4):
            mms.append(lambda E, n=n, et=et, tn=tn, pw=pwhich: E.matmul(
                psum_view(pw, tn, tn + 1),
                brow[n][:, et * P:(et + 1) * P],
                ones_row[:],
                start=False, stop=True))
        reads = [wsbuf, buf("brow" + n), buf("consts")] + \
                [buf(f"ar1_{dc}") for dc in range(ND)] + extra_reads
        pl.emit("tensor", mms, reads=reads, writes=banks(pwhich, 0, 4), sem="pe")

    def phi_evac(pwhich, out_tile, out_buf, s):
        pb = banks(pwhich, 0, 4)
        pl.emit("scalar", [lambda E, pw=pwhich, s=s: E.activation(
            out=r_t[:], in_=psum_view(pw, 0, 4), func=AF.Relu, scale=s)],
            reads=pb, writes=[buf("r_t")])
        pl.emit("vector", [lambda E, pw=pwhich: E.tensor_scalar(
            out=m_t[:], in0=psum_view(pw, 0, 4), scalar1=0.0, scalar2=None,
            op0=ALU.min)], reads=pb + [buf("r_t")], writes=[buf("m_t")])
        pl.emit("scalar", [lambda E: E.activation(
            out=m_t[:], in_=m_t[:], func=AF.Exp)],
            reads=[buf("m_t")], writes=[buf("m_t")])
        pl.emit("vector", [lambda E, out_tile=out_tile, s=s: E.scalar_tensor_tensor(
            out=out_tile[:], in0=m_t[:], scalar=s, in1=r_t[:],
            op0=ALU.mult, op1=ALU.add)],
            reads=[buf("m_t"), buf("r_t")], writes=[out_buf])

    for et in range(NE):
        pk = "A" if et % 2 == 0 else "B"
        pv = "B" if et % 2 == 0 else "A"
        phslot = et % 2
        if et % 2 == 0 and et + 2 < NE:
            for n in ("k", "v", "q"):
                load_eighth(n, (et // 2 + 1) % 2, et + 2)
        proj_mms(et, "k", pk, None, [])
        phi_evac(pk, ph_t[phslot], buf(f"ph_t{phslot}"), 1.0)
        proj_mms(et, "v", pv, None, [])
        pl.emit("vector", [lambda E, pv=pv, phslot=phslot: E.scalar_tensor_tensor(
            out=S_t[:], in0=psum_view(pv, 0, 4), scalar=1.0, in1=ph_t[phslot][:],
            op0=ALU.mult, op1=ALU.mult)],
            reads=banks(pv, 0, 4) + [buf(f"ph_t{phslot}")], writes=[buf("S_t")])
        ct = C_t[et % 2]
        ctu = buf(f"C_t{et % 2}")
        pl.emit("vector", [lambda E, ct=ct: E.tensor_tensor_scan(
            out=ct[:], data0=S_t[:], data1=S_t[:], initial=0.0,
            op0=ALU.add, op1=ALU.bypass)],
            reads=[buf("S_t")], writes=[ctu])
        pl.emit("vector", [lambda E, ct=ct, et=et: E.tensor_copy(
            out=totals[:, et:et + 1], in_=ct[:, T - 1:T])],
            reads=[ctu], writes=[buf("totals")])
        pl.dma("gpsimd", lambda et=et: cloc[et * P:(et + 1) * P, :],
               lambda ct=ct: ct[:], reads=[ctu], writes=[buf("cloc")])
        proj_mms(et, "q", pk, None, [buf(f"ph_t{phslot}")])
        phi_evac(pk, ph_t[phslot], buf(f"ph_t{phslot}"), SCALE)
        pl.dma("gpsimd", lambda et=et: phq[et * P:(et + 1) * P, :],
               lambda phslot=phslot: ph_t[phslot][:],
               reads=[buf(f"ph_t{phslot}")], writes=[buf("phq")])
        quant_pump(2)
        if et == 9 and not ag_done[0]:
            quant_flush()
            pl.emit("gpsimd", [lambda E: E.collective_compute(
                "AllGather", ALU.bypass, replica_groups=[list(range(8))],
                ins=[w1p.opt()], outs=[w1n.opt()])],
                reads=[buf("w1p")], writes=[buf("w1n")], sem="cc")
            pl.emit("gpsimd", [lambda E: E.collective_compute(
                "AllGather", ALU.bypass, replica_groups=[list(range(8))],
                ins=[w2p.opt()], outs=[w2n.opt()])],
                reads=[buf("w2p")], writes=[buf("w2n")], sem="cc")
            ag_done[0] = True
        if et > 9:
            pretr_pump(12)

    pl.dma("gpsimd", lambda: agi, lambda: totals[:], reads=[buf("totals")],
           writes=[buf("agi")])
    pl.emit("gpsimd", [lambda E: E.collective_compute(
        "AllGather", ALU.bypass, replica_groups=[list(range(8))],
        ins=[agi.opt()], outs=[ago.opt()])], reads=[buf("agi")], writes=[buf("ago")],
        sem="cc")
    pl.dma("gpsimd", lambda: agt.ap().rearrange("p (r c) -> p r c", c=16),
           lambda: ago.rearrange("(r p) c -> p r c", p=P),
           reads=[buf("ago")], writes=[buf("agt")])
    pl.emit("vector", [lambda E: E.tensor_tensor(
        out=maskd.ap().rearrange("p (c r) -> p r c", r=8),
        in0=agt.ap().rearrange("p (r c) -> p r c", c=16),
        in1=maskb.ap().unsqueeze(2).broadcast_to([P, 8, 16]),
        op=ALU.mult)], reads=[buf("agt"), buf("maskb")], writes=[buf("maskd")])
    pl.emit("vector", [lambda E: E.tensor_reduce(
        out=pcol.ap().unsqueeze(2),
        in_=maskd.ap().rearrange("p (c r) -> p c r", r=8),
        axis=AX.X, op=ALU.add)], reads=[buf("maskd")], writes=[buf("pcol")])

    scopeA1.close()
    pl.barrier()

    scopeA2 = ExitStack()
    e2 = scopeA2.enter_context
    cq = e2(nc.sbuf_tensor("cq", [P, NE * 512], BF16))
    pq = e2(nc.sbuf_tensor("pq", [P, NE * 512], BF16))
    ctx = e2(nc.sbuf_tensor("ctx", [P, NE * 512], BF16))
    xres = [e2(nc.sbuf_tensor(f"xres{i}", [P, D], F32)) for i in range(2)]
    ypre = e2(nc.sbuf_tensor("ypre", [P, D], F32))
    y1f_t = [e2(nc.sbuf_tensor(f"y1f_t{i}", [P, D], F32)) for i in range(2)]
    y1b_t = [e2(nc.sbuf_tensor(f"y1b_t{i}", [P, D], BF16)) for i in range(2)]
    st_s = e2(nc.sbuf_tensor("st_s", [P, 1], F32))
    st_q = e2(nc.sbuf_tensor("st_q", [P, 1], F32))
    st_m = e2(nc.sbuf_tensor("st_m", [P, 1], F32))
    st_m2 = e2(nc.sbuf_tensor("st_m2", [P, 1], F32))
    st_v = e2(nc.sbuf_tensor("st_v", [P, 1], F32))

    for ec in range(NE):
        pl.dma("scalar", lambda ec=ec: AR1[:, ec * T:(ec + 1) * T],
               lambda ec=ec: wT["o"][ec * P:(ec + 1) * P, :],
               reads=[buf("wTo")] + [buf(f"ar1_{ec}")],
               writes=[buf(f"ar1_{ec}")])

    for tq in range(4):
        pl.dma("scalar",
               lambda: cq.ap().rearrange("p (ec t) -> p ec t", t=512),
               lambda tq=tq: cloc.rearrange("(ec p) t -> p ec t", p=P)
               [:, :, tq * 512:(tq + 1) * 512],
               reads=[buf("cloc")], writes=[buf("cq")])
        pl.dma("scalar",
               lambda: pq.ap().rearrange("p (ec t) -> p ec t", t=512),
               lambda tq=tq: phq.rearrange("(ec p) t -> p ec t", p=P)
               [:, :, tq * 512:(tq + 1) * 512],
               reads=[buf("phq")], writes=[buf("pq")])
        ops = []
        for ec in range(NE):
            ops.append(lambda E, ec=ec: E.scalar_tensor_tensor(
                out=ctx[:, ec * 512:(ec + 1) * 512],
                in0=cq[:, ec * 512:(ec + 1) * 512],
                scalar=pcol[:, ec:ec + 1],
                in1=pq[:, ec * 512:(ec + 1) * 512],
                op0=ALU.add, op1=ALU.mult))
        pl.emit("vector", ops, reads=[buf("cq"), buf("pq"), buf("pcol")],
                writes=[buf("ctx")])

        for trel in range(4):
            tt = tq * 4 + trel
            pw = "A" if tt % 2 == 0 else "B"
            xr = xres[tt % 2]
            xru = buf(f"xres{tt % 2}")
            pl.dma("scalar", lambda xr=xr: xr[:],
                   lambda tt=tt: x_in[tt * P:(tt + 1) * P, :],
                   writes=[xru])
            mms = []
            for ec in range(NE):
                for dn in range(4):
                    mms.append(lambda E, ec=ec, dn=dn, trel=trel, pw=pw: E.matmul(
                        psum_view(pw, dn, dn + 1),
                        ctx[:, ec * 512 + trel * P: ec * 512 + (trel + 1) * P],
                        AR1[:, ec * T + dn * 512: ec * T + (dn + 1) * 512],
                        start=(ec == 0), stop=False))
            for dn in range(4):
                mms.append(lambda E, dn=dn, pw=pw: E.matmul(
                    psum_view(pw, dn, dn + 1),
                    ones_row[:, 0:P].bitcast(BF16) if False else ones_row[:, 0:P],
                    brow["o"][:, dn * 512:(dn + 1) * 512],
                    start=False, stop=True))
            pl.emit("tensor", mms,
                    reads=[buf("ctx"), buf("browo"), buf("consts")] +
                          [buf(f"ar1_{ec}") for ec in range(NE)],
                    writes=banks(pw, 0, 4), sem="pe")
            pb = banks(pw, 0, 4)
            pl.emit("vector", [lambda E, pw=pw, xr=xr: E.scalar_tensor_tensor(
                out=ypre[:], in0=psum_view(pw, 0, 4), scalar=1.0, in1=xr[:],
                op0=ALU.mult, op1=ALU.add, accum_out=st_s[:])],
                reads=pb + [xru], writes=[buf("ypre"), buf("st_s")])
            pl.emit("scalar", [lambda E, xr=xr: E.activation(
                out=xr[:], in_=ypre[:], func=AF.Square, accum_out=st_q[:])],
                reads=[buf("ypre")], writes=[xru, buf("st_q")])
            pl.emit("vector", [lambda E: E.tensor_scalar(
                out=st_m[:], in0=st_s[:], scalar1=INV_D, scalar2=None,
                op0=ALU.mult)], reads=[buf("st_s")], writes=[buf("st_m")])
            pl.emit("scalar", [lambda E: E.activation(
                out=st_m2[:], in_=st_m[:], func=AF.Square)],
                reads=[buf("st_m")], writes=[buf("st_m2")])
            pl.emit("vector", [lambda E: E.scalar_tensor_tensor(
                out=st_v[:], in0=st_q[:], scalar=INV_D, in1=st_m2[:],
                op0=ALU.mult, op1=ALU.subtract)],
                reads=[buf("st_q"), buf("st_m2")], writes=[buf("st_v")])
            pl.emit("scalar", [lambda E: E.activation(
                out=st_v[:], in_=st_v[:], func=AF.Ln, bias=c_eln[:])],
                reads=[buf("st_v"), buf("consts")], writes=[buf("st_v")])
            pl.emit("scalar", [lambda E: E.activation(
                out=st_v[:], in_=st_v[:], func=AF.Exp, scale=-0.5)],
                reads=[buf("st_v")], writes=[buf("st_v")])
            y1t = y1f_t[tt % 2]
            y1tu = buf(f"y1f_t{tt % 2}")
            pl.emit("vector", [lambda E, y1t=y1t: E.tensor_scalar(
                out=y1t[:], in0=ypre[:], scalar1=st_m[:], scalar2=st_v[:],
                op0=ALU.subtract, op1=ALU.mult)],
                reads=[buf("ypre"), buf("st_m"), buf("st_v")], writes=[y1tu])
            y1bt = y1b_t[tt % 2]
            y1btu = buf(f"y1b_t{tt % 2}")
            pl.emit("vector", [lambda E, y1t=y1t, y1bt=y1bt: E.tensor_copy(
                out=y1bt[:], in_=y1t[:])], reads=[y1tu], writes=[y1btu])
            pl.dma("gpsimd", lambda tt=tt: y1f[tt * P:(tt + 1) * P, :],
                   lambda y1t=y1t: y1t[:], reads=[y1tu], writes=[buf("y1f")])
            pl.dma("gpsimd", lambda tt=tt: y1b[tt * P:(tt + 1) * P, :],
                   lambda y1bt=y1bt: y1bt[:], reads=[y1btu], writes=[buf("y1b")])
            quant_pump(0)
            pretr_pump(9)

    scopeA2.close()
    scopeA.close()

    quant_flush()
    pretr_pump()
    assert pretr_ix[0] == len(pretr_todo)

    for dc in range(ND):
        bb = bounce[dc % 2]
        bbuf = buf(f"bounce{dc % 2}")
        pl.dma("sync", lambda bb=bb: bb[:],
               lambda dc=dc: y1b[:, dc * P:(dc + 1) * P],
               reads=[buf("y1b")], writes=[bbuf], transpose=True)
        pl.dma("scalar", lambda dc=dc: y1T[dc * P:(dc + 1) * P, :],
               lambda bb=bb: bb[:], reads=[bbuf], writes=[buf("y1T")])

    scopeB = ExitStack()
    eB = scopeB.enter_context
    AR2 = eB(nc.sbuf_tensor("AR2", [P, NH * 512], BF16))
    y1g = [eB(nc.sbuf_tensor("y1g0", [P, ND * 512], BF16))]
    wfcs = [eB(nc.sbuf_tensor(f"wfcs{i}", [P, ND * 256], BF16)) for i in range(2)]
    wprs = [eB(nc.sbuf_tensor(f"wprs{i}", [P, 8 * 512], BF16)) for i in range(2)]
    y1p = [eB(nc.sbuf_tensor(f"y1p{i}", [P, 512], F32)) for i in range(2)]
    out_t = [eB(nc.sbuf_tensor(f"out_t{i}", [P, D], F32)) for i in range(4)]
    rfc = eB(nc.sbuf_tensor("rfc", [P, 512], F32))
    gsqs = [eB(nc.sbuf_tensor(f"gsqs{i}", [P, 512], BF16)) for i in range(2)]
    sqp = eB(nc.sbuf_tensor("sqp", [P, 512], F32))
    rstd_row = eB(nc.sbuf_tensor("rstd_row", [1, 512], F32))
    rstdc = eB(nc.sbuf_tensor("rstdc", [P, 4], F32))
    acc_s = eB(nc.sbuf_tensor("acc_s", [P, 16], F32))
    acc_q = eB(nc.sbuf_tensor("acc_q", [P, 16], F32))
    bst_s = eB(nc.sbuf_tensor("bst_s", [P, 1], F32))
    bst_q = eB(nc.sbuf_tensor("bst_q", [P, 1], F32))
    bst_m = eB(nc.sbuf_tensor("bst_m", [P, 1], F32))
    bst_m2 = eB(nc.sbuf_tensor("bst_m2", [P, 1], F32))
    bst_v = eB(nc.sbuf_tensor("bst_v", [P, 1], F32))

    pl.barrier()

    for grp in range(4):
        gslot = 0
        pl.dma("sync",
               lambda gslot=gslot: y1g[gslot].ap().rearrange("p (dc t) -> p dc t", t=512),
               lambda grp=grp: y1T.rearrange("(dc p) t -> p dc t", p=P)
               [:, :, grp * 512:(grp + 1) * 512],
               reads=[buf("y1T")], writes=[buf(f"y1g{gslot}")])
        pend_ones = []
        for hpair in range(NH // 2):
            wslot = hpair % 2
            pl.dma("sync",
                   lambda wslot=wslot: wfcs[wslot].ap()
                   .rearrange("p (dc h) -> p dc h", h=256),
                   lambda hpair=hpair: wfcT.rearrange("(dc p) h -> p dc h", p=P)
                   [:, :, hpair * 256:(hpair + 1) * 256],
                   reads=[buf("wfcT")], writes=[buf(f"wfcs{wslot}")])
            for hrel in range(2):
                hc = hpair * 2 + hrel
                bank = hc % 4
                mms = []
                for dc in range(ND):
                    mms.append(lambda E, wslot=wslot, dc=dc, hrel=hrel, gslot=gslot,
                               bank=bank: E.matmul(
                                   psum_view("A", bank, bank + 1),
                                   wfcs[wslot][:, dc * 256 + hrel * P:
                                               dc * 256 + (hrel + 1) * P],
                                   y1g[gslot][:, dc * 512:(dc + 1) * 512],
                                   start=(dc == 0), stop=(dc == ND - 1)))
                pl.emit("tensor", mms,
                        reads=[buf(f"wfcs{wslot}"), buf(f"y1g{gslot}")],
                        writes=banks("A", bank, bank + 1), sem="pe")
                pbk = banks("A", bank, bank + 1)
                pl.emit("scalar", [lambda E, bank=bank: E.activation(
                    out=rfc[:], in_=psum_view("A", bank, bank + 1), func=AF.Relu)],
                    reads=pbk, writes=[buf("rfc")])
                pl.emit("scalar", [lambda E, hc=hc: E.activation(
                    out=AR2[:, hc * 512:(hc + 1) * 512], in_=rfc[:],
                    func=AF.Square)], reads=[buf("rfc")], writes=[buf(f"g{hc}")])
                pl.emit("scalar", [lambda E, hc=hc: E.activation(
                    out=gsqs[hc % 2][:], in_=AR2[:, hc * 512:(hc + 1) * 512],
                    func=AF.Square)], reads=[buf(f"g{hc}")],
                    writes=[buf(f"gsqs{hc % 2}")])
                pend_ones.append(hc)
                if len(pend_ones) > 2:
                    h0 = pend_ones.pop(0)
                    pl.emit("tensor", [lambda E, h0=h0: E.matmul(
                        psum_view("B", 0, 1), ones128[:], gsqs[h0 % 2][:],
                        start=(h0 == 0), stop=(h0 == NH - 1))],
                        reads=[buf(f"gsqs{h0 % 2}"), buf("consts")],
                        writes=banks("B", 0, 1), sem="pe")
        while pend_ones:
            h0 = pend_ones.pop(0)
            pl.emit("tensor", [lambda E, h0=h0: E.matmul(
                psum_view("B", 0, 1), ones128[:], gsqs[h0 % 2][:],
                start=(h0 == 0), stop=(h0 == NH - 1))],
                reads=[buf(f"gsqs{h0 % 2}"), buf("consts")],
                writes=banks("B", 0, 1), sem="pe")
        pl.emit("scalar", [lambda E: E.activation(
            out=rstd_row[:], in_=psum_view("B", 0, 1)[0:1, :],
            func=AF.Ln, scale=INV_H, bias=c_erm[0:1, :])],
            reads=banks("B", 0, 1) + [buf("consts")], writes=[buf("rstd_row")])
        pl.emit("scalar", [lambda E: E.activation(
            out=rstd_row[:], in_=rstd_row[:], func=AF.Exp, scale=-0.5)],
            reads=[buf("rstd_row")], writes=[buf("rstd_row")])
        tmms = []
        for trel in range(4):
            tmms.append(lambda E, trel=trel: E.matmul(
                pB[:, 512 + trel:512 + trel + 1],
                rstd_row[:, trel * P:(trel + 1) * P], ones1f[:],
                start=True, stop=True))
        pl.emit("tensor", tmms, reads=[buf("rstd_row"), buf("consts")],
                writes=banks("B", 1, 2), sem="pe")
        pl.emit("vector", [lambda E: E.tensor_copy(
            out=rstdc[:], in_=pB[:, 512:516])],
            reads=banks("B", 1, 2), writes=[buf("rstdc")])

        for dq in range(4):
            pw = "A" if dq % 2 == 0 else "B"
            for hb in range(8):
                wslot = hb % 2
                pl.dma("sync",
                       lambda wslot=wslot: wprs[wslot].ap()
                       .rearrange("p (hc d) -> p hc d", d=512),
                       lambda hb=hb, dq=dq: wprT.rearrange("(hc p) d -> p hc d", p=P)
                       [:, hb * 8:(hb + 1) * 8, dq * 512:(dq + 1) * 512],
                       reads=[buf("wprT")], writes=[buf(f"wprs{wslot}")])
                mms = []
                for hrel in range(8):
                    hc = hb * 8 + hrel
                    for trel in range(4):
                        mms.append(lambda E, hc=hc, hrel=hrel, trel=trel,
                                   wslot=wslot, pw=pw: E.matmul(
                                       psum_view(pw, trel, trel + 1),
                                       AR2[:, hc * 512 + trel * P:
                                           hc * 512 + (trel + 1) * P],
                                       wprs[wslot][:, hrel * 512:(hrel + 1) * 512],
                                       start=(hc == 0), stop=(hc == NH - 1)))
                pl.emit("tensor", mms,
                        reads=[buf(f"wprs{wslot}")] +
                              [buf(f"g{hb * 8 + i}") for i in range(8)],
                        writes=banks(pw, 0, 4), sem="pe")
            for trel in range(4):
                tt = grp * 4 + trel
                ys = y1p[trel % 2]
                ysu = buf(f"y1p{trel % 2}")
                pl.dma("scalar", lambda ys=ys: ys[:],
                       lambda tt=tt, dq=dq: y1f[tt * P:(tt + 1) * P,
                                                dq * 512:(dq + 1) * 512],
                       reads=[buf("y1f")], writes=[ysu])
                pbk = banks(pw, trel, trel + 1)
                oid = trel
                col = trel * 4 + dq
                pl.emit("vector", [lambda E, pw=pw, trel=trel, ys=ys, oid=oid,
                                   dq=dq, col=col: E.scalar_tensor_tensor(
                    out=out_t[oid][:, dq * 512:(dq + 1) * 512],
                    in0=psum_view(pw, trel, trel + 1),
                    scalar=rstdc[:, trel:trel + 1], in1=ys[:],
                    op0=ALU.mult, op1=ALU.add,
                    accum_out=acc_s[:, col:col + 1])],
                    reads=pbk + [ysu, buf("rstdc")],
                    writes=[buf(f"outp{oid}_{dq}"), buf("acc_s")])
                pl.emit("scalar", [lambda E, oid=oid, dq=dq, col=col: E.activation(
                    out=sqp[:], in_=out_t[oid][:, dq * 512:(dq + 1) * 512],
                    func=AF.Square, accum_out=acc_q[:, col:col + 1])],
                    reads=[buf(f"outp{oid}_{dq}")],
                    writes=[buf("sqp"), buf("acc_q")])
        for trel in range(4):
            tt = grp * 4 + trel
            oid = trel
            opieces = [buf(f"outp{oid}_{dq}") for dq in range(4)]
            pl.emit("vector", [lambda E, trel=trel: E.tensor_reduce(
                out=bst_s.ap().unsqueeze(2),
                in_=acc_s.ap()[:, trel * 4:(trel + 1) * 4].unsqueeze(1),
                axis=AX.X, op=ALU.add)],
                reads=[buf("acc_s")], writes=[buf("bst_s")])
            pl.emit("vector", [lambda E, trel=trel: E.tensor_reduce(
                out=bst_q.ap().unsqueeze(2),
                in_=acc_q.ap()[:, trel * 4:(trel + 1) * 4].unsqueeze(1),
                axis=AX.X, op=ALU.add)],
                reads=[buf("acc_q")], writes=[buf("bst_q")])
            pl.emit("vector", [lambda E: E.tensor_scalar(
                out=bst_m[:], in0=bst_s[:], scalar1=INV_D, scalar2=None,
                op0=ALU.mult)], reads=[buf("bst_s")], writes=[buf("bst_m")])
            pl.emit("scalar", [lambda E: E.activation(
                out=bst_m2[:], in_=bst_m[:], func=AF.Square)],
                reads=[buf("bst_m")], writes=[buf("bst_m2")])
            pl.emit("vector", [lambda E: E.scalar_tensor_tensor(
                out=bst_v[:], in0=bst_q[:], scalar=INV_D, in1=bst_m2[:],
                op0=ALU.mult, op1=ALU.subtract)],
                reads=[buf("bst_q"), buf("bst_m2")], writes=[buf("bst_v")])
            pl.emit("scalar", [lambda E: E.activation(
                out=bst_v[:], in_=bst_v[:], func=AF.Ln, bias=c_eln[:])],
                reads=[buf("bst_v"), buf("consts")], writes=[buf("bst_v")])
            pl.emit("scalar", [lambda E: E.activation(
                out=bst_v[:], in_=bst_v[:], func=AF.Exp, scale=-0.5)],
                reads=[buf("bst_v")], writes=[buf("bst_v")])
            pl.emit("vector", [lambda E, oid=oid: E.tensor_scalar(
                out=out_t[oid][:], in0=out_t[oid][:], scalar1=bst_m[:],
                scalar2=bst_v[:], op0=ALU.subtract, op1=ALU.mult)],
                reads=opieces + [buf("bst_m"), buf("bst_v")],
                writes=opieces)
            pl.dma("scalar", lambda tt=tt: out_hbm[tt * P:(tt + 1) * P, :],
                   lambda oid=oid: out_t[oid][:],
                   reads=opieces, writes=[buf("outdma")])

    scopeB.close()

    final_dc = pl.cnt.get("dc", 0)

    with nc.Block() as block:

        def run(E, q):
            for item in q:
                if item[0] == "w":
                    E.wait_ge(sems[item[1]], item[2])
                elif item[0] == "p":
                    item[1](E)
                else:
                    item[1](E).then_inc(sems[item[2]], item[3])

        @block.sync
        def _(E):
            run(E, pl.q["sync"])
            E.wait_ge(sems["dc"], final_dc)

        @block.scalar
        def _(E):
            run(E, pl.q["scalar"])

        @block.vector
        def _(E):
            run(E, pl.q["vector"])

        @block.tensor
        def _(E):
            run(E, pl.q["tensor"])

        @block.gpsimd
        def _(E):
            run(E, pl.q["gpsimd"])

    st.close()
    return nc


_NC_CACHE = {}


def _get_nc():
    if "nc" not in _NC_CACHE:
        _NC_CACHE["nc"] = build_nc()
    return _NC_CACHE["nc"]


def kernel(L0_attn_x, L0_attn_Wq, L0_attn_bq, L0_attn_Wk, L0_attn_bk,
           L0_attn_Wv, L0_attn_bv, L0_attn_Wo, L0_attn_bo,
           L0_ffn_W_fc, L0_ffn_W_proj):
    x = np.ascontiguousarray(np.asarray(L0_attn_x, dtype=np.float32))
    ws = {n: np.asarray(w, dtype=np.float32)
          for n, w in (("q", L0_attn_Wq), ("k", L0_attn_Wk),
                       ("v", L0_attn_Wv), ("o", L0_attn_Wo))}
    bs = {n: np.ascontiguousarray(np.asarray(b, dtype=np.float32).reshape(1, D))
          for n, b in (("q", L0_attn_bq), ("k", L0_attn_bk),
                       ("v", L0_attn_bv), ("o", L0_attn_bo))}
    wfc = np.asarray(L0_ffn_W_fc, dtype=np.float32)
    wpr = np.asarray(L0_ffn_W_proj, dtype=np.float32)

    in_maps = []
    for c in range(8):
        b, j = c // 4, c % 4
        m = np.zeros((P, 8), dtype=np.float32)
        for r in range(8):
            if r // 4 == b and (r % 4) < j:
                m[:, r] = 1.0
        im = {"x": np.ascontiguousarray(x[b, j * T:(j + 1) * T, :]),
              "mask": m,
              "wfcq": np.ascontiguousarray(wfc[c * (HID // 8):(c + 1) * (HID // 8), :]),
              "wprq": np.ascontiguousarray(wpr[c * (D // 8):(c + 1) * (D // 8), :])}
        for n in ("q", "k", "v", "o"):
            im["w" + n] = np.ascontiguousarray(ws[n][c * (D // 8):(c + 1) * (D // 8), :])
            im["b" + n] = bs[n]
        in_maps.append(im)

    nc = _get_nc()
    res = run_bass_kernel_spmd(nc, in_maps, core_ids=list(range(8)))
    out = np.empty((2, 8192, D), dtype=np.float32)
    for c in range(8):
        b, j = c // 4, c % 4
        out[b, j * T:(j + 1) * T, :] = np.asarray(res.results[c]["out"],
                                                  dtype=np.float32)
    return out


# version 07
# speedup vs baseline: 1.4307x; 1.0866x over previous
import numpy as np
import ml_dtypes
from contextlib import ExitStack

import concourse.bass as bass
import concourse.mybir as mybir
from concourse.bass_utils import run_bass_kernel_spmd

F32 = mybir.dt.float32
BF16 = mybir.dt.bfloat16
AF = mybir.ActivationFunctionType
ALU = mybir.AluOpType
AX = mybir.AxisListType
bfdt = ml_dtypes.bfloat16

P = 128
D = 2048
T = 2048
HID = 8192
ND = D // P
NE = D // P
NT = T // P
NH = HID // P
SCALE = float(np.float32(128 ** -0.5))
MAGIC = float(np.float32(1.5 * 2 ** 23))
EPS_LN = 1e-5
EPS_RMS = float(np.float32(np.finfo(np.float32).eps))
INV_D = 1.0 / 2048.0
INV_H = 1.0 / 8192.0
INV_G = 1.0 / 64.0

ENGINES = ("sync", "scalar", "vector", "tensor", "gpsimd")
QSEM = {"sync": "ds", "scalar": "dc", "gpsimd": "dg"}
ESEM = {"tensor": "pe", "vector": "dv", "scalar": "ac", "gpsimd": "po"}


class Buf:
    __slots__ = ("w", "readers")

    def __init__(self):
        self.w = None
        self.readers = []


class Plan:
    def __init__(self):
        self.q = {e: [] for e in ENGINES}
        self.cnt = {}
        self.sat = {e: {} for e in ENGINES}

    def _wait(self, eng, sem, val):
        if val <= 0 or self.sat[eng].get(sem, 0) >= val:
            return
        self.sat[eng][sem] = val
        self.q[eng].append(("w", sem, val))

    def emit(self, eng, builders, reads=(), writes=(), sem=None, inc=1):
        need = {}
        for b in reads:
            if b.w:
                s, v = b.w
                need[s] = max(need.get(s, 0), v)
        for b in writes:
            if b.w:
                s, v = b.w
                need[s] = max(need.get(s, 0), v)
            for s, v in b.readers:
                need[s] = max(need.get(s, 0), v)
        for s, v in need.items():
            self._wait(eng, s, v)
        if sem is None:
            sem = QSEM.get(eng) if False else ESEM.get(eng)
        self.cnt[sem] = self.cnt.get(sem, 0) + inc
        tok = (sem, self.cnt[sem])
        if not isinstance(builders, (list, tuple)):
            builders = [builders]
        for i, b in enumerate(builders):
            if i == len(builders) - 1:
                self.q[eng].append(("i", b, sem, inc))
            else:
                self.q[eng].append(("p", b))
        for b in writes:
            b.w = tok
            b.readers = []
        for b in reads:
            b.readers.append(tok)
        return tok

    def dma(self, queue, out_ap_f, in_ap_f, reads=(), writes=(), **kw):
        sem = QSEM[queue]
        return self.emit(
            queue,
            [lambda E, o=out_ap_f, i=in_ap_f, kw=kw: E.dma_start(out=o(), in_=i(), **kw)],
            reads=reads, writes=writes, sem=sem, inc=16,
        )

    def barrier(self):
        snap = dict(self.cnt)
        for eng in ENGINES:
            for s, v in snap.items():
                self._wait(eng, s, v)


def build_nc():
    nc = bass.Bass("TRN2", target_bir_lowering=False, detect_race_conditions=False)
    pl = Plan()

    x_in = nc.dram_tensor("x", [T, D], F32, kind="ExternalInput").ap()
    w_in = {n: nc.dram_tensor("w" + n, [D, D // 8], F32, kind="ExternalInput").ap()
            for n in ("q", "k", "v", "o")}
    b_in = {n: nc.dram_tensor("b" + n, [1, D], F32, kind="ExternalInput").ap()
            for n in ("q", "k", "v", "o")}
    wfcq_in = nc.dram_tensor("wfcq", [HID // 8, D], F32, kind="ExternalInput").ap()
    wprq_in = nc.dram_tensor("wprq", [D // 8, HID], F32, kind="ExternalInput").ap()
    mask_in = nc.dram_tensor("mask", [P, 8], F32, kind="ExternalInput").ap()
    out_hbm = nc.dram_tensor("out", [T, D], F32, kind="ExternalOutput").ap()

    xb = nc.dram_tensor("xb", [T, D], BF16).ap()
    wcol = {n: nc.dram_tensor("wc" + n, [D, D // 8], BF16).ap() for n in ("q", "k", "v", "o")}
    wTp = {n: nc.dram_tensor("wTp" + n, [D // 8, D], BF16).ap() for n in ("q", "k", "v", "o")}
    wT = {n: nc.dram_tensor("wT" + n, [D, D], BF16).ap() for n in ("q", "k", "v", "o")}
    w1p = nc.dram_tensor("w1p", [HID // 8, D], BF16).ap()
    w2p = nc.dram_tensor("w2p", [D // 8, HID], BF16).ap()
    w1n = nc.dram_tensor("w1n", [HID, D], BF16).ap()
    w2n = nc.dram_tensor("w2n", [D, HID], BF16).ap()
    wfcT = nc.dram_tensor("wfcT", [D, HID], BF16).ap()
    wprT = nc.dram_tensor("wprT", [HID, D], BF16).ap()
    cloc = nc.dram_tensor("cloc", [D, T], BF16).ap()
    phq = nc.dram_tensor("phq", [D, T], BF16).ap()
    y1f = nc.dram_tensor("y1f", [T, D], F32).ap()
    y1b = nc.dram_tensor("y1b", [T, D], BF16).ap()
    y1T = nc.dram_tensor("y1T", [D, T], BF16).ap()
    agi = nc.dram_tensor("agi", [P, 16], F32).ap()
    ago = nc.dram_tensor("ago", [8 * P, 16], F32).ap()

    st = ExitStack()
    e = st.enter_context

    sems = {}
    for s in ("pe", "dv", "ac", "po", "cc", "ds", "dc", "dg"):
        sems[s] = e(nc.semaphore(s))

    bounce = [e(nc.sbuf_tensor(f"bounce{i}", [P, D], BF16)) for i in range(2)]
    qw_wb = [e(nc.sbuf_tensor(f"qwb{i}", [P, D], BF16)) for i in range(2)]
    qw_t = [e(nc.sbuf_tensor(f"qwt{i}", [P, D], BF16)) for i in range(2)]
    qgs = [e(nc.sbuf_tensor(f"qgs{i}", [P, 32], F32)) for i in range(2)]
    qsc = [e(nc.sbuf_tensor(f"qsc{i}", [P, 32], BF16)) for i in range(2)]
    qrc = [e(nc.sbuf_tensor(f"qrc{i}", [P, 32], F32)) for i in range(2)]
    totals = e(nc.sbuf_tensor("totals", [P, 16], F32))
    agt = e(nc.sbuf_tensor("agt", [P, 8 * 16], F32))
    maskb = e(nc.sbuf_tensor("maskb", [P, 8], F32))
    maskd = e(nc.sbuf_tensor("maskd", [P, 16 * 8], F32))
    pcol = e(nc.sbuf_tensor("pcol", [P, 16], F32))
    ones_row = e(nc.sbuf_tensor("ones_row", [1, 512], BF16))
    ones128 = e(nc.sbuf_tensor("ones128", [P, P], BF16))
    ones1f = e(nc.sbuf_tensor("ones1f", [1, 1], F32))
    c_eln = e(nc.sbuf_tensor("c_eln", [P, 1], F32))
    c_erm = e(nc.sbuf_tensor("c_erm", [P, 1], F32))

    pA = e(nc.psum_tensor("pA", [P, D], F32))
    pB = e(nc.psum_tensor("pB", [P, D], F32))
    pbank = [Buf() for _ in range(8)]

    def psum_view(which, lo, hi):
        t_ = pA if which == "A" else pB
        return t_[:, lo * 512:hi * 512]

    def banks(which, lo, hi):
        off = 0 if which == "A" else 4
        return [pbank[off + i] for i in range(lo, hi)]

    B_ = {}

    def buf(name):
        if name not in B_:
            B_[name] = Buf()
        return B_[name]

    pl.emit("vector", [
        lambda E: E.memset(ones_row[:], 1.0),
        lambda E: E.memset(ones128[:], 1.0),
        lambda E: E.memset(ones1f[:], 1.0),
        lambda E: E.memset(c_eln[:], EPS_LN),
        lambda E: E.memset(c_erm[:], EPS_RMS),
    ], writes=[buf("consts")], sem="dv")
    pl.dma("scalar", lambda: maskb[:], lambda: mask_in, writes=[buf("maskb")])

    brow = {n: e(nc.sbuf_tensor(f"brow{n}", [1, D], BF16)) for n in ("q", "k", "v", "o")}
    for n in ("q", "k", "v", "o"):
        pl.dma("gpsimd", lambda n=n: brow[n][:], lambda n=n: b_in[n], writes=[buf("brow" + n)])

    for n in ("k", "v", "q", "o"):
        pl.dma("gpsimd", lambda n=n: wcol[n], lambda n=n: w_in[n],
               writes=[buf(f"wc{n}")])
        if n == "k":
            for ci in range(4):
                r0, r1 = ci * (T // 4), (ci + 1) * (T // 4)
                pl.dma("gpsimd", lambda r0=r0, r1=r1: xb[r0:r1, :],
                       lambda r0=r0, r1=r1: x_in[r0:r1, :],
                       writes=[buf("xb")])

    scopeA = ExitStack()
    eA = scopeA.enter_context
    AR1 = eA(nc.sbuf_tensor("AR1", [P, ND * T], BF16))

    def pretranspose_weight(n):
        for i in range(2):
            bb = bounce[i % 2]
            bbuf = buf(f"bounce{i % 2}")
            pl.dma("sync",
                   lambda bb=bb: bb[:],
                   lambda n=n, i=i: wcol[n][:, i * P:(i + 1) * P],
                   reads=[buf(f"wc{n}")], writes=[bbuf], transpose=True)
            pl.dma("sync",
                   lambda n=n, i=i: wTp[n][i * P:(i + 1) * P, :],
                   lambda bb=bb: bb[:],
                   reads=[bbuf], writes=[buf(f"wTp{n}")])
        pl.emit("gpsimd", [lambda E, n=n: E.collective_compute(
            "AllGather", ALU.bypass, replica_groups=[list(range(8))],
            ins=[wTp[n].opt()], outs=[wT[n].opt()])],
            reads=[buf(f"wTp{n}")], writes=[buf(f"wT{n}")], sem="cc")

    pretranspose_weight("k")
    pretranspose_weight("v")
    for dc in range(ND):
        pl.dma("sync",
               lambda dc=dc: AR1[:, dc * T:(dc + 1) * T],
               lambda dc=dc: xb[:, dc * P:(dc + 1) * P],
               reads=[buf("xb")], writes=[buf(f"ar1_{dc}")], transpose=True)
    pretranspose_weight("q")
    pretranspose_weight("o")

    qcount = [0]

    def quant_stage1(src_ap_f, i):
        wbb, tb = qw_wb[i], qw_t[i]
        gs, sc, rc = qgs[i], qsc[i], qrc[i]
        wbu, tbu = buf(f"qwb{i}"), buf(f"qwt{i}")
        gsu, scu, rcu = buf(f"qgs{i}"), buf(f"qsc{i}"), buf(f"qrc{i}")
        pl.dma("gpsimd", lambda wbb=wbb: wbb[:], src_ap_f, writes=[wbu])
        pl.emit("vector", [lambda E, wbb=wbb, gs=gs: E.tensor_reduce(
            out=gs.ap().unsqueeze(2),
            in_=wbb.ap().rearrange("p (g j) -> p g j", j=64),
            axis=AX.X, op=ALU.add, apply_absolute_value=True)],
            reads=[wbu], writes=[gsu])
        pl.emit("vector", [lambda E, gs=gs, sc=sc: E.tensor_scalar(
            out=sc[:], in0=gs[:], scalar1=INV_G, scalar2=1e-8,
            op0=ALU.mult, op1=ALU.max)], reads=[gsu], writes=[scu])
        pl.emit("vector", [lambda E, sc=sc, rc=rc: E.reciprocal(out=rc[:], in_=sc[:])],
                reads=[scu], writes=[rcu])
        pl.emit("gpsimd", [lambda E, wbb=wbb, rc=rc, tb=tb: E.tensor_tensor(
            out=tb.ap().rearrange("p (g j) -> p g j", j=64),
            in0=wbb.ap().rearrange("p (g j) -> p g j", j=64),
            in1=rc.ap().unsqueeze(2).broadcast_to([P, 32, 64]),
            op=ALU.mult)], reads=[wbu, rcu], writes=[tbu], sem="po")

    def quant_stage2(dst_ap_f, tag, i):
        wbb, tb = qw_wb[i], qw_t[i]
        sc = qsc[i]
        wbu, tbu, scu = buf(f"qwb{i}"), buf(f"qwt{i}"), buf(f"qsc{i}")
        pl.emit("vector", [lambda E, tb=tb: E.tensor_scalar(
            out=tb[:], in0=tb[:], scalar1=MAGIC, scalar2=MAGIC,
            op0=ALU.add, op1=ALU.subtract)], reads=[tbu], writes=[tbu])
        pl.emit("vector", [lambda E, tb=tb: E.tensor_scalar(
            out=tb[:], in0=tb[:], scalar1=1.0, scalar2=-1.0,
            op0=ALU.min, op1=ALU.max)], reads=[tbu], writes=[tbu])
        pl.emit("vector", [lambda E, tb=tb, sc=sc, wbb=wbb: E.tensor_tensor(
            out=wbb.ap().rearrange("p (g j) -> p g j", j=64),
            in0=tb.ap().rearrange("p (g j) -> p g j", j=64),
            in1=sc.ap().unsqueeze(2).broadcast_to([P, 32, 64]),
            op=ALU.mult)], reads=[tbu, scu], writes=[wbu])
        pl.dma("scalar", dst_ap_f, lambda wbb=wbb: wbb[:],
               reads=[wbu], writes=[buf(tag)])

    fc_blocks = [(lambda r=r: wfcq_in[r * P:(r + 1) * P, :],
                  lambda r=r: w1p[r * P:(r + 1) * P, :]) for r in range(HID // 8 // P)]
    pr_blocks = []
    for r in range(D // 8 // P):
        for hq in range(4):
            pr_blocks.append(
                (lambda r=r, hq=hq: wprq_in[r * P:(r + 1) * P, hq * D:(hq + 1) * D],
                 lambda r=r, hq=hq: w2p[r * P:(r + 1) * P, hq * D:(hq + 1) * D]))
    quant_todo = [(s, d, "w1p") for (s, d) in fc_blocks] + \
                 [(s, d, "w2p") for (s, d) in pr_blocks]
    quant_s1 = [0]
    quant_s2 = [0]

    def quant_pump(k):
        pretr_pump(6)
        for _ in range(k):
            if quant_s1[0] < len(quant_todo):
                i = quant_s1[0]
                quant_stage1(quant_todo[i][0], i % 2)
                quant_s1[0] += 1
            if quant_s2[0] < quant_s1[0] - 1 or (
                    quant_s1[0] == len(quant_todo) and quant_s2[0] < quant_s1[0]):
                i = quant_s2[0]
                quant_stage2(quant_todo[i][1], quant_todo[i][2], i % 2)
                quant_s2[0] += 1
            pretr_pump()

    def quant_flush():
        while quant_s1[0] < len(quant_todo) or quant_s2[0] < len(quant_todo):
            if quant_s1[0] < len(quant_todo):
                i = quant_s1[0]
                quant_stage1(quant_todo[i][0], i % 2)
                quant_s1[0] += 1
            if quant_s2[0] < quant_s1[0]:
                i = quant_s2[0]
                quant_stage2(quant_todo[i][1], quant_todo[i][2], i % 2)
                quant_s2[0] += 1
            pretr_pump()

    pretr_todo = []
    for hq in range(4):
        for dc in range(ND):
            pretr_todo.append((
                lambda dc=dc, hq=hq: w1n[hq * D:(hq + 1) * D, dc * P:(dc + 1) * P],
                lambda dc=dc, hq=hq: wfcT[dc * P:(dc + 1) * P, hq * D:(hq + 1) * D],
                "w1n", "wfcT"))
    for hc in range(NH):
        pretr_todo.append((
            lambda hc=hc: w2n[:, hc * P:(hc + 1) * P],
            lambda hc=hc: wprT[hc * P:(hc + 1) * P, :],
            "w2n", "wprT"))
    pretr_ix = [0]
    pretr_n = [0]
    ag_done = [False]

    def pretr_pump(k=999):
        while ag_done[0] and k > 0 and pretr_ix[0] < len(pretr_todo):
            inf, outf, stag, dtag = pretr_todo[pretr_ix[0]]
            pretr_ix[0] += 1
            k -= 1
            bb = bounce[pretr_n[0] % 2]
            bbuf = buf(f"bounce{pretr_n[0] % 2}")
            pretr_n[0] += 1
            pl.dma("sync", lambda bb=bb: bb[:], inf,
                   reads=[buf(stag)], writes=[bbuf], transpose=True)
            pl.dma("sync", outf, lambda bb=bb: bb[:],
                   reads=[bbuf], writes=[buf(dtag)])

    scopeA1 = ExitStack()
    e1 = scopeA1.enter_context
    wstream = {n: e1(nc.sbuf_tensor(f"wst{n}", [P, 2 * ND * 256], BF16))
               for n in ("k", "v", "q")}
    r_t = e1(nc.sbuf_tensor("r_t", [P, T], F32))
    m_t = e1(nc.sbuf_tensor("m_t", [P, T], F32))
    S_t = e1(nc.sbuf_tensor("S_t", [P, T], BF16))
    C_t = [e1(nc.sbuf_tensor(f"C_t{i}", [P, T], BF16)) for i in range(2)]
    ph_t = [e1(nc.sbuf_tensor(f"ph_t{i}", [P, T], BF16)) for i in range(2)]

    def wst_view(n, slot, dc, e0, width):
        base = slot * ND * 256 + dc * 256
        return wstream[n][:, base + e0: base + e0 + width]

    def load_eighth(n, slot, et0):
        pl.dma("scalar",
               lambda n=n, slot=slot: wstream[n][:, slot * ND * 256:(slot + 1) * ND * 256]
               .rearrange("p (dc e) -> p dc e", e=256),
               lambda n=n, et0=et0: wT[n].rearrange("(dc p) e -> p dc e", p=P)
               [:, :, et0 * P:(et0 + 2) * P],
               reads=[buf(f"wT{n}")], writes=[buf(f"wst{n}{slot}")])

    for n in ("k", "v", "q"):
        load_eighth(n, 0, 0)

    def proj_mms(et, n, pwhich, phi_scale, extra_reads):
        slot = (et // 2) % 2
        erel = (et % 2) * P
        wsbuf = buf(f"wst{n}{slot}")
        mms = []
        for dc in range(ND):
            for tn in range(4):
                mms.append(lambda E, n=n, slot=slot, dc=dc, erel=erel, tn=tn,
                           pw=pwhich: E.matmul(
                               psum_view(pw, tn, tn + 1),
                               wst_view(n, slot, dc, erel, P),
                               AR1[:, dc * T + tn * 512: dc * T + (tn + 1) * 512],
                               start=(dc == 0), stop=False))
        for tn in range(4):
            mms.append(lambda E, n=n, et=et, tn=tn, pw=pwhich: E.matmul(
                psum_view(pw, tn, tn + 1),
                brow[n][:, et * P:(et + 1) * P],
                ones_row[:],
                start=False, stop=True))
        reads = [wsbuf, buf("brow" + n), buf("consts")] + \
                [buf(f"ar1_{dc}") for dc in range(ND)] + extra_reads
        pl.emit("tensor", mms, reads=reads, writes=banks(pwhich, 0, 4), sem="pe")

    def phi_evac(pwhich, out_tile, out_buf, s):
        pb = banks(pwhich, 0, 4)
        pl.emit("scalar", [lambda E, pw=pwhich, s=s: E.activation(
            out=r_t[:], in_=psum_view(pw, 0, 4), func=AF.Relu, scale=s)],
            reads=pb, writes=[buf("r_t")])
        pl.emit("vector", [lambda E, pw=pwhich: E.tensor_scalar(
            out=m_t[:], in0=psum_view(pw, 0, 4), scalar1=0.0, scalar2=None,
            op0=ALU.min)], reads=pb + [buf("r_t")], writes=[buf("m_t")])
        pl.emit("scalar", [lambda E: E.activation(
            out=m_t[:], in_=m_t[:], func=AF.Exp)],
            reads=[buf("m_t")], writes=[buf("m_t")])
        pl.emit("vector", [lambda E, out_tile=out_tile, s=s: E.scalar_tensor_tensor(
            out=out_tile[:], in0=m_t[:], scalar=s, in1=r_t[:],
            op0=ALU.mult, op1=ALU.add)],
            reads=[buf("m_t"), buf("r_t")], writes=[out_buf])

    for et in range(NE):
        pk = "A" if et % 2 == 0 else "B"
        pv = "B" if et % 2 == 0 else "A"
        phslot = et % 2
        if et % 2 == 0 and et + 2 < NE:
            for n in ("k", "v", "q"):
                load_eighth(n, (et // 2 + 1) % 2, et + 2)
        proj_mms(et, "k", pk, None, [])
        phi_evac(pk, ph_t[phslot], buf(f"ph_t{phslot}"), 1.0)
        proj_mms(et, "v", pv, None, [])
        pl.emit("vector", [lambda E, pv=pv, phslot=phslot: E.scalar_tensor_tensor(
            out=S_t[:], in0=psum_view(pv, 0, 4), scalar=1.0, in1=ph_t[phslot][:],
            op0=ALU.mult, op1=ALU.mult)],
            reads=banks(pv, 0, 4) + [buf(f"ph_t{phslot}")], writes=[buf("S_t")])
        ct = C_t[et % 2]
        ctu = buf(f"C_t{et % 2}")
        pl.emit("vector", [lambda E, ct=ct: E.tensor_tensor_scan(
            out=ct[:], data0=S_t[:], data1=S_t[:], initial=0.0,
            op0=ALU.add, op1=ALU.bypass)],
            reads=[buf("S_t")], writes=[ctu])
        pl.emit("vector", [lambda E, ct=ct, et=et: E.tensor_copy(
            out=totals[:, et:et + 1], in_=ct[:, T - 1:T])],
            reads=[ctu], writes=[buf("totals")])
        pl.dma("gpsimd", lambda et=et: cloc[et * P:(et + 1) * P, :],
               lambda ct=ct: ct[:], reads=[ctu], writes=[buf("cloc")])
        proj_mms(et, "q", pk, None, [buf(f"ph_t{phslot}")])
        phi_evac(pk, ph_t[phslot], buf(f"ph_t{phslot}"), SCALE)
        pl.dma("gpsimd", lambda et=et: phq[et * P:(et + 1) * P, :],
               lambda phslot=phslot: ph_t[phslot][:],
               reads=[buf(f"ph_t{phslot}")], writes=[buf("phq")])
        quant_pump(3)
        if et == 5 and not ag_done[0]:
            quant_flush()
            pl.emit("gpsimd", [lambda E: E.collective_compute(
                "AllGather", ALU.bypass, replica_groups=[list(range(8))],
                ins=[w1p.opt()], outs=[w1n.opt()])],
                reads=[buf("w1p")], writes=[buf("w1n")], sem="cc")
            pl.emit("gpsimd", [lambda E: E.collective_compute(
                "AllGather", ALU.bypass, replica_groups=[list(range(8))],
                ins=[w2p.opt()], outs=[w2n.opt()])],
                reads=[buf("w2p")], writes=[buf("w2n")], sem="cc")
            ag_done[0] = True
        if et > 5:
            pretr_pump(14)

    pl.dma("gpsimd", lambda: agi, lambda: totals[:], reads=[buf("totals")],
           writes=[buf("agi")])
    pl.emit("gpsimd", [lambda E: E.collective_compute(
        "AllGather", ALU.bypass, replica_groups=[list(range(8))],
        ins=[agi.opt()], outs=[ago.opt()])], reads=[buf("agi")], writes=[buf("ago")],
        sem="cc")
    pl.dma("gpsimd", lambda: agt.ap().rearrange("p (r c) -> p r c", c=16),
           lambda: ago.rearrange("(r p) c -> p r c", p=P),
           reads=[buf("ago")], writes=[buf("agt")])
    pl.emit("vector", [lambda E: E.tensor_tensor(
        out=maskd.ap().rearrange("p (c r) -> p r c", r=8),
        in0=agt.ap().rearrange("p (r c) -> p r c", c=16),
        in1=maskb.ap().unsqueeze(2).broadcast_to([P, 8, 16]),
        op=ALU.mult)], reads=[buf("agt"), buf("maskb")], writes=[buf("maskd")])
    pl.emit("vector", [lambda E: E.tensor_reduce(
        out=pcol.ap().unsqueeze(2),
        in_=maskd.ap().rearrange("p (c r) -> p c r", r=8),
        axis=AX.X, op=ALU.add)], reads=[buf("maskd")], writes=[buf("pcol")])

    scopeA1.close()
    pl.barrier()

    scopeA2 = ExitStack()
    e2 = scopeA2.enter_context
    cq = e2(nc.sbuf_tensor("cq", [P, NE * 512], BF16))
    pq = e2(nc.sbuf_tensor("pq", [P, NE * 512], BF16))
    ctx = e2(nc.sbuf_tensor("ctx", [P, NE * 512], BF16))
    xres = [e2(nc.sbuf_tensor(f"xres{i}", [P, D], F32)) for i in range(2)]
    ypre = e2(nc.sbuf_tensor("ypre", [P, D], F32))
    y1f_t = [e2(nc.sbuf_tensor(f"y1f_t{i}", [P, D], F32)) for i in range(2)]
    y1b_t = [e2(nc.sbuf_tensor(f"y1b_t{i}", [P, D], BF16)) for i in range(2)]
    st_s = e2(nc.sbuf_tensor("st_s", [P, 1], F32))
    st_q = e2(nc.sbuf_tensor("st_q", [P, 1], F32))
    st_m = e2(nc.sbuf_tensor("st_m", [P, 1], F32))
    st_m2 = e2(nc.sbuf_tensor("st_m2", [P, 1], F32))
    st_v = e2(nc.sbuf_tensor("st_v", [P, 1], F32))

    for ec in range(NE):
        pl.dma("scalar", lambda ec=ec: AR1[:, ec * T:(ec + 1) * T],
               lambda ec=ec: wT["o"][ec * P:(ec + 1) * P, :],
               reads=[buf("wTo")] + [buf(f"ar1_{ec}")],
               writes=[buf(f"ar1_{ec}")])

    for tq in range(4):
        pl.dma("scalar",
               lambda: cq.ap().rearrange("p (ec t) -> p ec t", t=512),
               lambda tq=tq: cloc.rearrange("(ec p) t -> p ec t", p=P)
               [:, :, tq * 512:(tq + 1) * 512],
               reads=[buf("cloc")], writes=[buf("cq")])
        pl.dma("scalar",
               lambda: pq.ap().rearrange("p (ec t) -> p ec t", t=512),
               lambda tq=tq: phq.rearrange("(ec p) t -> p ec t", p=P)
               [:, :, tq * 512:(tq + 1) * 512],
               reads=[buf("phq")], writes=[buf("pq")])
        ops = []
        for ec in range(NE):
            ops.append(lambda E, ec=ec: E.scalar_tensor_tensor(
                out=ctx[:, ec * 512:(ec + 1) * 512],
                in0=cq[:, ec * 512:(ec + 1) * 512],
                scalar=pcol[:, ec:ec + 1],
                in1=pq[:, ec * 512:(ec + 1) * 512],
                op0=ALU.add, op1=ALU.mult))
        pl.emit("vector", ops, reads=[buf("cq"), buf("pq"), buf("pcol")],
                writes=[buf("ctx")])

        for trel in range(4):
            tt = tq * 4 + trel
            pw = "A" if tt % 2 == 0 else "B"
            xr = xres[tt % 2]
            xru = buf(f"xres{tt % 2}")
            pl.dma("scalar", lambda xr=xr: xr[:],
                   lambda tt=tt: x_in[tt * P:(tt + 1) * P, :],
                   writes=[xru])
            mms = []
            for ec in range(NE):
                for dn in range(4):
                    mms.append(lambda E, ec=ec, dn=dn, trel=trel, pw=pw: E.matmul(
                        psum_view(pw, dn, dn + 1),
                        ctx[:, ec * 512 + trel * P: ec * 512 + (trel + 1) * P],
                        AR1[:, ec * T + dn * 512: ec * T + (dn + 1) * 512],
                        start=(ec == 0), stop=False))
            for dn in range(4):
                mms.append(lambda E, dn=dn, pw=pw: E.matmul(
                    psum_view(pw, dn, dn + 1),
                    ones_row[:, 0:P].bitcast(BF16) if False else ones_row[:, 0:P],
                    brow["o"][:, dn * 512:(dn + 1) * 512],
                    start=False, stop=True))
            pl.emit("tensor", mms,
                    reads=[buf("ctx"), buf("browo"), buf("consts")] +
                          [buf(f"ar1_{ec}") for ec in range(NE)],
                    writes=banks(pw, 0, 4), sem="pe")
            pb = banks(pw, 0, 4)
            pl.emit("vector", [lambda E, pw=pw, xr=xr: E.scalar_tensor_tensor(
                out=ypre[:], in0=psum_view(pw, 0, 4), scalar=1.0, in1=xr[:],
                op0=ALU.mult, op1=ALU.add, accum_out=st_s[:])],
                reads=pb + [xru], writes=[buf("ypre"), buf("st_s")])
            pl.emit("scalar", [lambda E, xr=xr: E.activation(
                out=xr[:], in_=ypre[:], func=AF.Square, accum_out=st_q[:])],
                reads=[buf("ypre")], writes=[xru, buf("st_q")])
            pl.emit("vector", [lambda E: E.tensor_scalar(
                out=st_m[:], in0=st_s[:], scalar1=INV_D, scalar2=None,
                op0=ALU.mult)], reads=[buf("st_s")], writes=[buf("st_m")])
            pl.emit("scalar", [lambda E: E.activation(
                out=st_m2[:], in_=st_m[:], func=AF.Square)],
                reads=[buf("st_m")], writes=[buf("st_m2")])
            pl.emit("vector", [lambda E: E.scalar_tensor_tensor(
                out=st_v[:], in0=st_q[:], scalar=INV_D, in1=st_m2[:],
                op0=ALU.mult, op1=ALU.subtract)],
                reads=[buf("st_q"), buf("st_m2")], writes=[buf("st_v")])
            pl.emit("scalar", [lambda E: E.activation(
                out=st_v[:], in_=st_v[:], func=AF.Ln, bias=c_eln[:])],
                reads=[buf("st_v"), buf("consts")], writes=[buf("st_v")])
            pl.emit("scalar", [lambda E: E.activation(
                out=st_v[:], in_=st_v[:], func=AF.Exp, scale=-0.5)],
                reads=[buf("st_v")], writes=[buf("st_v")])
            y1t = y1f_t[tt % 2]
            y1tu = buf(f"y1f_t{tt % 2}")
            pl.emit("vector", [lambda E, y1t=y1t: E.tensor_scalar(
                out=y1t[:], in0=ypre[:], scalar1=st_m[:], scalar2=st_v[:],
                op0=ALU.subtract, op1=ALU.mult)],
                reads=[buf("ypre"), buf("st_m"), buf("st_v")], writes=[y1tu])
            y1bt = y1b_t[tt % 2]
            y1btu = buf(f"y1b_t{tt % 2}")
            pl.emit("vector", [lambda E, y1t=y1t, y1bt=y1bt: E.tensor_copy(
                out=y1bt[:], in_=y1t[:])], reads=[y1tu], writes=[y1btu])
            pl.dma("gpsimd", lambda tt=tt: y1f[tt * P:(tt + 1) * P, :],
                   lambda y1t=y1t: y1t[:], reads=[y1tu], writes=[buf("y1f")])
            pl.dma("gpsimd", lambda tt=tt: y1b[tt * P:(tt + 1) * P, :],
                   lambda y1bt=y1bt: y1bt[:], reads=[y1btu], writes=[buf("y1b")])
            quant_pump(0)
            pretr_pump(9)

    scopeA2.close()
    scopeA.close()

    quant_flush()
    pretr_pump()
    assert pretr_ix[0] == len(pretr_todo)

    for dc in range(ND):
        bb = bounce[dc % 2]
        bbuf = buf(f"bounce{dc % 2}")
        pl.dma("sync", lambda bb=bb: bb[:],
               lambda dc=dc: y1b[:, dc * P:(dc + 1) * P],
               reads=[buf("y1b")], writes=[bbuf], transpose=True)
        pl.dma("scalar", lambda dc=dc: y1T[dc * P:(dc + 1) * P, :],
               lambda bb=bb: bb[:], reads=[bbuf], writes=[buf("y1T")])

    scopeB = ExitStack()
    eB = scopeB.enter_context
    AR2 = eB(nc.sbuf_tensor("AR2", [P, NH * 512], BF16))
    y1g = [eB(nc.sbuf_tensor("y1g0", [P, ND * 512], BF16))]
    wfcs = [eB(nc.sbuf_tensor(f"wfcs{i}", [P, ND * 256], BF16)) for i in range(2)]
    wprs = [eB(nc.sbuf_tensor(f"wprs{i}", [P, 8 * 512], BF16)) for i in range(2)]
    y1p = [eB(nc.sbuf_tensor(f"y1p{i}", [P, 512], F32)) for i in range(2)]
    out_t = [eB(nc.sbuf_tensor(f"out_t{i}", [P, D], F32)) for i in range(4)]
    rfc = eB(nc.sbuf_tensor("rfc", [P, 512], F32))
    gsqs = [eB(nc.sbuf_tensor(f"gsqs{i}", [P, 512], BF16)) for i in range(2)]
    sqp = eB(nc.sbuf_tensor("sqp", [P, 512], F32))
    rstd_row = eB(nc.sbuf_tensor("rstd_row", [1, 512], F32))
    rstdc = eB(nc.sbuf_tensor("rstdc", [P, 4], F32))
    acc_s = eB(nc.sbuf_tensor("acc_s", [P, 16], F32))
    acc_q = eB(nc.sbuf_tensor("acc_q", [P, 16], F32))
    bst_s = eB(nc.sbuf_tensor("bst_s", [P, 1], F32))
    bst_q = eB(nc.sbuf_tensor("bst_q", [P, 1], F32))
    bst_m = eB(nc.sbuf_tensor("bst_m", [P, 1], F32))
    bst_m2 = eB(nc.sbuf_tensor("bst_m2", [P, 1], F32))
    bst_v = eB(nc.sbuf_tensor("bst_v", [P, 1], F32))

    pl.barrier()

    for grp in range(4):
        gslot = 0
        pl.dma("sync",
               lambda gslot=gslot: y1g[gslot].ap().rearrange("p (dc t) -> p dc t", t=512),
               lambda grp=grp: y1T.rearrange("(dc p) t -> p dc t", p=P)
               [:, :, grp * 512:(grp + 1) * 512],
               reads=[buf("y1T")], writes=[buf(f"y1g{gslot}")])
        pend_ones = []
        for hpair in range(NH // 2):
            wslot = hpair % 2
            pl.dma("sync",
                   lambda wslot=wslot: wfcs[wslot].ap()
                   .rearrange("p (dc h) -> p dc h", h=256),
                   lambda hpair=hpair: wfcT.rearrange("(dc p) h -> p dc h", p=P)
                   [:, :, hpair * 256:(hpair + 1) * 256],
                   reads=[buf("wfcT")], writes=[buf(f"wfcs{wslot}")])
            for hrel in range(2):
                hc = hpair * 2 + hrel
                bank = hc % 4
                mms = []
                for dc in range(ND):
                    mms.append(lambda E, wslot=wslot, dc=dc, hrel=hrel, gslot=gslot,
                               bank=bank: E.matmul(
                                   psum_view("A", bank, bank + 1),
                                   wfcs[wslot][:, dc * 256 + hrel * P:
                                               dc * 256 + (hrel + 1) * P],
                                   y1g[gslot][:, dc * 512:(dc + 1) * 512],
                                   start=(dc == 0), stop=(dc == ND - 1)))
                pl.emit("tensor", mms,
                        reads=[buf(f"wfcs{wslot}"), buf(f"y1g{gslot}")],
                        writes=banks("A", bank, bank + 1), sem="pe")
                pbk = banks("A", bank, bank + 1)
                pl.emit("scalar", [lambda E, bank=bank: E.activation(
                    out=rfc[:], in_=psum_view("A", bank, bank + 1), func=AF.Relu)],
                    reads=pbk, writes=[buf("rfc")])
                pl.emit("scalar", [lambda E, hc=hc: E.activation(
                    out=AR2[:, hc * 512:(hc + 1) * 512], in_=rfc[:],
                    func=AF.Square)], reads=[buf("rfc")], writes=[buf(f"g{hc}")])
                pl.emit("scalar", [lambda E, hc=hc: E.activation(
                    out=gsqs[hc % 2][:], in_=AR2[:, hc * 512:(hc + 1) * 512],
                    func=AF.Square)], reads=[buf(f"g{hc}")],
                    writes=[buf(f"gsqs{hc % 2}")])
                pend_ones.append(hc)
                if len(pend_ones) > 2:
                    h0 = pend_ones.pop(0)
                    pl.emit("tensor", [lambda E, h0=h0: E.matmul(
                        psum_view("B", 0, 1), ones128[:], gsqs[h0 % 2][:],
                        start=(h0 == 0), stop=(h0 == NH - 1))],
                        reads=[buf(f"gsqs{h0 % 2}"), buf("consts")],
                        writes=banks("B", 0, 1), sem="pe")
        while pend_ones:
            h0 = pend_ones.pop(0)
            pl.emit("tensor", [lambda E, h0=h0: E.matmul(
                psum_view("B", 0, 1), ones128[:], gsqs[h0 % 2][:],
                start=(h0 == 0), stop=(h0 == NH - 1))],
                reads=[buf(f"gsqs{h0 % 2}"), buf("consts")],
                writes=banks("B", 0, 1), sem="pe")
        pl.emit("scalar", [lambda E: E.activation(
            out=rstd_row[:], in_=psum_view("B", 0, 1)[0:1, :],
            func=AF.Ln, scale=INV_H, bias=c_erm[0:1, :])],
            reads=banks("B", 0, 1) + [buf("consts")], writes=[buf("rstd_row")])
        pl.emit("scalar", [lambda E: E.activation(
            out=rstd_row[:], in_=rstd_row[:], func=AF.Exp, scale=-0.5)],
            reads=[buf("rstd_row")], writes=[buf("rstd_row")])
        tmms = []
        for trel in range(4):
            tmms.append(lambda E, trel=trel: E.matmul(
                pB[:, 512 + trel:512 + trel + 1],
                rstd_row[:, trel * P:(trel + 1) * P], ones1f[:],
                start=True, stop=True))
        pl.emit("tensor", tmms, reads=[buf("rstd_row"), buf("consts")],
                writes=banks("B", 1, 2), sem="pe")
        pl.emit("vector", [lambda E: E.tensor_copy(
            out=rstdc[:], in_=pB[:, 512:516])],
            reads=banks("B", 1, 2), writes=[buf("rstdc")])

        for dq in range(4):
            pw = "A" if dq % 2 == 0 else "B"
            for hb in range(8):
                wslot = hb % 2
                pl.dma("sync",
                       lambda wslot=wslot: wprs[wslot].ap()
                       .rearrange("p (hc d) -> p hc d", d=512),
                       lambda hb=hb, dq=dq: wprT.rearrange("(hc p) d -> p hc d", p=P)
                       [:, hb * 8:(hb + 1) * 8, dq * 512:(dq + 1) * 512],
                       reads=[buf("wprT")], writes=[buf(f"wprs{wslot}")])
                mms = []
                for hrel in range(8):
                    hc = hb * 8 + hrel
                    for trel in range(4):
                        mms.append(lambda E, hc=hc, hrel=hrel, trel=trel,
                                   wslot=wslot, pw=pw: E.matmul(
                                       psum_view(pw, trel, trel + 1),
                                       AR2[:, hc * 512 + trel * P:
                                           hc * 512 + (trel + 1) * P],
                                       wprs[wslot][:, hrel * 512:(hrel + 1) * 512],
                                       start=(hc == 0), stop=(hc == NH - 1)))
                pl.emit("tensor", mms,
                        reads=[buf(f"wprs{wslot}")] +
                              [buf(f"g{hb * 8 + i}") for i in range(8)],
                        writes=banks(pw, 0, 4), sem="pe")
            for trel in range(4):
                tt = grp * 4 + trel
                ys = y1p[trel % 2]
                ysu = buf(f"y1p{trel % 2}")
                pl.dma("scalar", lambda ys=ys: ys[:],
                       lambda tt=tt, dq=dq: y1f[tt * P:(tt + 1) * P,
                                                dq * 512:(dq + 1) * 512],
                       reads=[buf("y1f")], writes=[ysu])
                pbk = banks(pw, trel, trel + 1)
                oid = trel
                col = trel * 4 + dq
                pl.emit("vector", [lambda E, pw=pw, trel=trel, ys=ys, oid=oid,
                                   dq=dq, col=col: E.scalar_tensor_tensor(
                    out=out_t[oid][:, dq * 512:(dq + 1) * 512],
                    in0=psum_view(pw, trel, trel + 1),
                    scalar=rstdc[:, trel:trel + 1], in1=ys[:],
                    op0=ALU.mult, op1=ALU.add,
                    accum_out=acc_s[:, col:col + 1])],
                    reads=pbk + [ysu, buf("rstdc")],
                    writes=[buf(f"outp{oid}_{dq}"), buf("acc_s")])
                pl.emit("scalar", [lambda E, oid=oid, dq=dq, col=col: E.activation(
                    out=sqp[:], in_=out_t[oid][:, dq * 512:(dq + 1) * 512],
                    func=AF.Square, accum_out=acc_q[:, col:col + 1])],
                    reads=[buf(f"outp{oid}_{dq}")],
                    writes=[buf("sqp"), buf("acc_q")])
        for trel in range(4):
            tt = grp * 4 + trel
            oid = trel
            opieces = [buf(f"outp{oid}_{dq}") for dq in range(4)]
            pl.emit("vector", [lambda E, trel=trel: E.tensor_reduce(
                out=bst_s.ap().unsqueeze(2),
                in_=acc_s.ap()[:, trel * 4:(trel + 1) * 4].unsqueeze(1),
                axis=AX.X, op=ALU.add)],
                reads=[buf("acc_s")], writes=[buf("bst_s")])
            pl.emit("vector", [lambda E, trel=trel: E.tensor_reduce(
                out=bst_q.ap().unsqueeze(2),
                in_=acc_q.ap()[:, trel * 4:(trel + 1) * 4].unsqueeze(1),
                axis=AX.X, op=ALU.add)],
                reads=[buf("acc_q")], writes=[buf("bst_q")])
            pl.emit("vector", [lambda E: E.tensor_scalar(
                out=bst_m[:], in0=bst_s[:], scalar1=INV_D, scalar2=None,
                op0=ALU.mult)], reads=[buf("bst_s")], writes=[buf("bst_m")])
            pl.emit("scalar", [lambda E: E.activation(
                out=bst_m2[:], in_=bst_m[:], func=AF.Square)],
                reads=[buf("bst_m")], writes=[buf("bst_m2")])
            pl.emit("vector", [lambda E: E.scalar_tensor_tensor(
                out=bst_v[:], in0=bst_q[:], scalar=INV_D, in1=bst_m2[:],
                op0=ALU.mult, op1=ALU.subtract)],
                reads=[buf("bst_q"), buf("bst_m2")], writes=[buf("bst_v")])
            pl.emit("scalar", [lambda E: E.activation(
                out=bst_v[:], in_=bst_v[:], func=AF.Ln, bias=c_eln[:])],
                reads=[buf("bst_v"), buf("consts")], writes=[buf("bst_v")])
            pl.emit("scalar", [lambda E: E.activation(
                out=bst_v[:], in_=bst_v[:], func=AF.Exp, scale=-0.5)],
                reads=[buf("bst_v")], writes=[buf("bst_v")])
            pl.emit("vector", [lambda E, oid=oid: E.tensor_scalar(
                out=out_t[oid][:], in0=out_t[oid][:], scalar1=bst_m[:],
                scalar2=bst_v[:], op0=ALU.subtract, op1=ALU.mult)],
                reads=opieces + [buf("bst_m"), buf("bst_v")],
                writes=opieces)
            pl.dma("scalar", lambda tt=tt: out_hbm[tt * P:(tt + 1) * P, :],
                   lambda oid=oid: out_t[oid][:],
                   reads=opieces, writes=[buf("outdma")])

    scopeB.close()

    final_dc = pl.cnt.get("dc", 0)

    with nc.Block() as block:

        def run(E, q):
            for item in q:
                if item[0] == "w":
                    E.wait_ge(sems[item[1]], item[2])
                elif item[0] == "p":
                    item[1](E)
                else:
                    item[1](E).then_inc(sems[item[2]], item[3])

        @block.sync
        def _(E):
            run(E, pl.q["sync"])
            E.wait_ge(sems["dc"], final_dc)

        @block.scalar
        def _(E):
            run(E, pl.q["scalar"])

        @block.vector
        def _(E):
            run(E, pl.q["vector"])

        @block.tensor
        def _(E):
            run(E, pl.q["tensor"])

        @block.gpsimd
        def _(E):
            run(E, pl.q["gpsimd"])

    st.close()
    return nc


_NC_CACHE = {}


def _get_nc():
    if "nc" not in _NC_CACHE:
        _NC_CACHE["nc"] = build_nc()
    return _NC_CACHE["nc"]


def kernel(L0_attn_x, L0_attn_Wq, L0_attn_bq, L0_attn_Wk, L0_attn_bk,
           L0_attn_Wv, L0_attn_bv, L0_attn_Wo, L0_attn_bo,
           L0_ffn_W_fc, L0_ffn_W_proj):
    x = np.ascontiguousarray(np.asarray(L0_attn_x, dtype=np.float32))
    ws = {n: np.asarray(w, dtype=np.float32)
          for n, w in (("q", L0_attn_Wq), ("k", L0_attn_Wk),
                       ("v", L0_attn_Wv), ("o", L0_attn_Wo))}
    bs = {n: np.ascontiguousarray(np.asarray(b, dtype=np.float32).reshape(1, D))
          for n, b in (("q", L0_attn_bq), ("k", L0_attn_bk),
                       ("v", L0_attn_bv), ("o", L0_attn_bo))}
    wfc = np.asarray(L0_ffn_W_fc, dtype=np.float32)
    wpr = np.asarray(L0_ffn_W_proj, dtype=np.float32)

    in_maps = []
    for c in range(8):
        b, j = c // 4, c % 4
        m = np.zeros((P, 8), dtype=np.float32)
        for r in range(8):
            if r // 4 == b and (r % 4) < j:
                m[:, r] = 1.0
        im = {"x": np.ascontiguousarray(x[b, j * T:(j + 1) * T, :]),
              "mask": m,
              "wfcq": np.ascontiguousarray(wfc[c * (HID // 8):(c + 1) * (HID // 8), :]),
              "wprq": np.ascontiguousarray(wpr[c * (D // 8):(c + 1) * (D // 8), :])}
        for n in ("q", "k", "v", "o"):
            im["w" + n] = np.ascontiguousarray(ws[n][:, c * (D // 8):(c + 1) * (D // 8)])
            im["b" + n] = bs[n]
        in_maps.append(im)

    nc = _get_nc()
    res = run_bass_kernel_spmd(nc, in_maps, core_ids=list(range(8)))
    out = np.empty((2, 8192, D), dtype=np.float32)
    for c in range(8):
        b, j = c // 4, c % 4
        out[b, j * T:(j + 1) * T, :] = np.asarray(res.results[c]["out"],
                                                  dtype=np.float32)
    return out
